# version 18
# baseline (speedup 1.0000x reference)
import numpy as np
import ml_dtypes
from contextlib import ExitStack

import concourse.bass as bass
import concourse.mybir as mybir
from concourse.bass_utils import run_bass_kernel_spmd

F32 = mybir.dt.float32
BF16 = mybir.dt.bfloat16
AF = mybir.ActivationFunctionType
ALU = mybir.AluOpType
AX = mybir.AxisListType
BF = ml_dtypes.bfloat16

NCORES = 8
TCORE = 2048
HALO = 128
DM = 1024
DI = 2048
DS = 16
KCONV = 4
NE = 16
FF = 512
FS = 1024

TT_A = 512
NT_A = TCORE // TT_A
TT_C1 = 256
NT_C1 = TCORE // TT_C1
TT_C2 = 512
NT_C2 = TCORE // TT_C2
TT_C3 = 256
NT_C3 = TCORE // TT_C3

ARENA_BYTES = 194560
ENGINES = ("sync", "tensor", "vector", "scalar", "gpsimd")


class Sem:
    __slots__ = ("h", "n", "name")

    def __init__(self, h, name):
        self.h = h
        self.n = 0
        self.name = name

    def w(self):
        return (self, self.n)


class Sched:
    def __init__(self):
        self.items = {e: [] for e in ENGINES}
        self._fence = {e: () for e in ENGINES}

    def fence(self, waits):
        ws = tuple(w for w in waits if w and w[1] > 0)
        for e in ENGINES:
            self._fence[e] = self._fence[e] + ws

    def add(self, eng, fn, waits=(), inc=None, amt=1):
        ws = [w for w in waits if w is not None and w[1] > 0]
        if self._fence[eng]:
            ws.extend(self._fence[eng])
            self._fence[eng] = ()
        if inc is not None:
            inc.n += amt
        self.items[eng].append((fn, tuple(ws), inc, amt))

    def play(self, eng_name, eng):
        waited = {}
        for fn, ws, inc, amt in self.items[eng_name]:
            for (s, v) in ws:
                k = id(s)
                if waited.get(k, 0) < v:
                    eng.wait_ge(s.h, v)
                    waited[k] = v
            ins = fn(eng)
            if inc is not None:
                ins.then_inc(inc.h, amt)


class Arena:
    def __init__(self, ap, nbytes):
        self.ap = ap
        self.nbytes = nbytes
        self.off = 0
        self.marks = []

    def push(self):
        self.marks.append(self.off)

    def pop(self):
        self.off = self.marks.pop()

    def alloc(self, free_elems, dtype, align=64):
        isz = 2 if dtype == BF16 else 4
        self.off = (self.off + align - 1) // align * align
        nb = free_elems * isz
        assert self.off + nb <= self.nbytes, (
            f"arena overflow: off={self.off} need={nb} cap={self.nbytes}")
        view = self.ap[:, self.off:self.off + nb].bitcast(dtype)
        self.off += nb
        return view


class Bank:
    __slots__ = ("ps", "wr", "rd")

    def __init__(self, ps):
        self.ps = ps
        self.wr = None
        self.rd = []


def build_program(debug=0):
    nc = bass.Bass("TRN2", target_bir_lowering=False)
    d_in = {}

    def din(name, shape, dtype=F32):
        d_in[name] = nc.dram_tensor(name, list(shape), dtype,
                                    kind="ExternalInput").ap()
        return d_in[name]

    x_tok = din("x_tok", [HALO + TCORE, DM])
    w_ip = din("w_ip", [DM, 2 * DI])
    w_dt = din("w_dt", [DI, DI])
    w_bc = din("w_bc", [DI, 2 * DS])
    w_out = din("w_out", [DI, DM])
    w_gate = din("w_gate", [DM, NE])
    w1e = din("w1e", [NE, DM, FF])
    w3e = din("w3e", [NE, DM, FF])
    w2e = din("w2e", [NE, FF, DM])
    w1s = din("w1s", [DM, FS])
    w3s = din("w3s", [DM, FS])
    w2s = din("w2s", [FS, DM])
    convw_r = din("convw_r", [128, 16 * KCONV])
    convb_r = din("convb_r", [128, 16])
    dtb_r = din("dtb_r", [128, 16])
    alog_r = din("alog_r", [128, 16 * DS])
    dp_r = din("dp_r", [128, 16])
    ident_in = din("ident_in", [128, 128])
    ones_row = din("ones_row", [1, 128])
    ones_col_bf = din("ones_col_bf", [128, 1], BF16)
    sel16_in = din("sel16_in", [16, 16 * 128])
    sel8_in = din("sel8_in", [128, 8])
    fconst_in = din("fconst_in", [128, 4])

    out_t = nc.dram_tensor("out", [TCORE, DM], F32, kind="ExternalOutput").ap()

    sp_xs = nc.dram_tensor("sp_xs", [16, 128, TCORE], BF16).ap()
    sp_sz = nc.dram_tensor("sp_sz", [16, 128, TCORE], BF16).ap()
    sp_xfm = nc.dram_tensor("sp_xfm", [8, 128, TCORE], F32).ap()
    sp_dt = nc.dram_tensor("sp_dt", [16, 128, TCORE], F32).ap()
    sp_dx = nc.dram_tensor("sp_dx", [16, 128, TCORE], F32).ap()
    sp_y = nc.dram_tensor("sp_y", [16, 128, TCORE], F32).ap()
    sp_x2 = nc.dram_tensor("sp_x2", [8, 128, TCORE], F32).ap()
    wexp_bf_dram = nc.dram_tensor("wexp_bf", [NE, 3, 128, 8 * FF], BF16).ap()
    wsh_bf_dram = nc.dram_tensor("wsh_bf", [3, 128, 8 * FS], BF16).ap()
    cc_in = nc.dram_tensor("cc_in", [128, 16 * DS], F32).ap()
    cc_out = nc.dram_tensor("cc_out", [NCORES * 128, 16 * DS], F32,
                            addr_space="Shared").ap()

    S = Sched()
    ctx = ExitStack()
    sems = {}
    for name in ("ld", "st", "pe", "ve", "se", "po", "cc", "st1", "sty",
                 "stx2", "sto", "ldg", "stsz", "stxs", "ldw", "lde0", "lde1",
                 "ldpc", "stpc"):
        sems[name] = Sem(ctx.enter_context(nc.semaphore(name)), name)
    (ld, st, pe, ve, se, po, ccs, st1, sty, stx2, sto, ldg, stsz, stxs,
     ldw, lde0, lde1, ldpc, stpc) = (
        sems[k] for k in ("ld", "st", "pe", "ve", "se", "po", "cc", "st1",
                          "sty", "stx2", "sto", "ldg", "stsz", "stxs", "ldw",
                          "lde0", "lde1", "ldpc", "stpc"))

    arena_h = ctx.enter_context(nc.sbuf_tensor("arena", [128, ARENA_BYTES],
                                               mybir.dt.uint8))
    A = Arena(arena_h[:, :], ARENA_BYTES)

    banks = []
    for i in range(8):
        ps = ctx.enter_context(nc.psum_tensor(f"ps{i}", [128, 512], F32))
        banks.append(Bank(ps[:, :]))

    def dma_in(dst, src, waits=()):
        S.add("sync", lambda e, d=dst, s=src: e.dma_start(out=d, in_=s),
              waits=waits, inc=ld, amt=16)
        return ld.w()

    def dma_out(dst, src, waits=(), sem=None):
        sem = sem if sem is not None else st
        S.add("sync", lambda e, d=dst, s=src: e.dma_start(out=d, in_=s),
              waits=waits, inc=sem, amt=16)
        return sem.w()

    def pop(fn, waits=()):
        S.add("gpsimd", fn, waits=waits, inc=po)
        return po.w()

    def vop(fn, waits=()):
        S.add("vector", fn, waits=waits, inc=ve)
        return ve.w()

    def sop(fn, waits=()):
        S.add("scalar", fn, waits=waits, inc=se)
        return se.w()

    def mms(bank, pairs, N, out_rows=128, waits=()):
        n = len(pairs)
        out_ap = bank.ps[0:out_rows, 0:N]
        for i in range(n):
            lt, rh = pairs[i]
            w = (list(waits) + list(bank.rd)) if i == 0 else ()
            S.add("tensor",
                  (lambda e, l=lt, r=rh, o=out_ap, s=(i == 0), p=(i == n - 1):
                   e.matmul(out=o, lhsT=l, rhs=r, start=s, stop=p)),
                  waits=w, inc=pe if i == n - 1 else None)
        bank.wr = pe.w()
        bank.rd = []
        return bank.wr

    def transpose_into(bank, in_ap, ident_ap, waits=()):
        pr, fr = in_ap.shape[0], in_ap.shape[1]
        w = list(waits) + list(bank.rd)
        S.add("tensor",
              (lambda e, i=in_ap, d=ident_ap, o=bank.ps[0:fr, 0:pr]:
               e.transpose(out=o, in_=i, identity=d)),
              waits=w, inc=pe)
        bank.wr = pe.w()
        bank.rd = []
        return bank.wr

    def read_bank(bank, eng, fn, waits=()):
        s = {"vector": ve, "scalar": se}[eng]
        S.add(eng, fn, waits=list(waits) + ([bank.wr] if bank.wr else []), inc=s)
        bank.rd.append(s.w())
        return s.w()

    buf_rd = {}

    def load_cast(hbm_ap, rows, cols, dst_bf, stage, chunk_cols=None, waits=()):
        cw = chunk_cols or cols
        last = buf_rd.get(id(stage))
        extra = list(waits)
        for kc in range(rows // 128):
            for c0 in range(0, cols, cw):
                S.add("sync", lambda e, d=stage[:, 0:cw],
                      s2=hbm_ap[kc * 128:(kc + 1) * 128, c0:c0 + cw]:
                      e.dma_start(out=d, in_=s2),
                      waits=([last] if last else []) + extra, inc=ldw, amt=16)
                w = ldw.w()
                extra = []
                last = vop(lambda e, o=dst_bf[:, kc * cols + c0: kc * cols + c0 + cw],
                           i=stage[:, 0:cw]: e.tensor_copy(out=o, in_=i), waits=[w])
        buf_rd[id(stage)] = last
        return last

    ident = A.alloc(128, F32)
    sel16 = A.alloc(16 * 128, F32)
    sel8 = A.alloc(8, F32)
    onesr = A.alloc(128, F32)
    onescb = A.alloc(1, BF16)
    A_sb = A.alloc(16 * DS, F32)
    convw = A.alloc(16 * KCONV, F32)
    convb = A.alloc(16, F32)
    dtbc = A.alloc(16, F32)
    dpc = A.alloc(16, F32)
    state = A.alloc(16 * DS, F32)

    fconst = A.alloc(4, F32)
    dma_in(fconst, fconst_in)
    for _i, _v in enumerate((1e-5, 1.0, -0.5, -1.0)):
        nc.const_aps.register(F32, _v, fconst[:, _i:_i + 1])
    dma_in(ident, ident_in)
    dma_in(sel16[0:16, :], sel16_in)
    dma_in(sel8, sel8_in)
    dma_in(onesr[0:1, :], ones_row)
    dma_in(onescb, ones_col_bf)
    dma_in(convw, convw_r)
    dma_in(convb, convb_r)
    dma_in(dtbc, dtb_r)
    dma_in(dpc, dp_r)
    wal = dma_in(A_sb, alog_r)
    wse = sop(lambda e: e.activation(out=A_sb, in_=A_sb, func=AF.Exp), waits=[wal])
    wA = vop(lambda e: e.tensor_scalar_mul(out=A_sb, in0=A_sb, scalar1=-1.0),
             waits=[wse])
    consts_ready = [ld.w(), wA]
    ln_eps = 1e-5

    pass

    A.push()
    w_ip_bf = A.alloc(8 * 2 * DI, BF16)
    stage8 = A.alloc(2048, F32)
    xbf = stage8.bitcast(BF16)[:, 0:8 * TT_A]
    x_tok_sb = [A.alloc(DM, F32), A.alloc(DM, F32)]
    ldx = None
    x_fm = A.alloc(8 * TT_A, F32)
    xp_sb = A.alloc(16 * (TT_A + 3), F32)
    xs_f = A.alloc(16 * TT_A, F32)
    szxq = A.alloc(16 * TT_A, BF16)
    lnrow = szxq.bitcast(F32)[:, 4 * TT_A: 8 * TT_A]

    S.fence(consts_ready)
    buf_rd["ld0"] = ld.w()
    wipr = load_cast(w_ip, DM, 2 * DI, w_ip_bf, stage8, chunk_cols=2048)

    def emit_ln(x_ap, tw, xout_bf, lnrow_ap, ttsz, rw_waits, sq_waits, ip_waits):
        xsq = szxq[:, 0:8 * tw]
        for db in range(8):
            vop(lambda e, o=xout_bf[:, db * tw:(db + 1) * tw],
                i=x_ap[:, db * tw:(db + 1) * tw]: e.tensor_copy(out=o, in_=i),
                waits=(list(rw_waits) + [buf_rd.get("a1xbf_pe")])
                if db == 0 else ())
        xbf_ready = ve.w()
        for db in range(8):
            sop(lambda e, o=xsq[:, db * tw:(db + 1) * tw],
                i=x_ap[:, db * tw:(db + 1) * tw]: e.activation(
                    out=o, in_=i, func=AF.Square),
                waits=(list(rw_waits) + list(sq_waits)) if db == 0 else ())
        xsq_ready = se.w()
        s1b, s2b = banks[2], banks[3]
        mms(s1b, [(onescb, xout_bf[:, db * tw:(db + 1) * tw]) for db in range(8)],
            tw, out_rows=1, waits=[xbf_ready])
        mms(s2b, [(onescb, xsq[:, db * tw:(db + 1) * tw]) for db in range(8)],
            tw, out_rows=1, waits=[xsq_ready])
        m_row = lnrow_ap[0:1, 0:tw]
        var_row = lnrow_ap[0:1, ttsz:ttsz + tw]
        rstd_row = lnrow_ap[0:1, 2 * ttsz:2 * ttsz + tw]
        mrs_row = lnrow_ap[0:1, 3 * ttsz:3 * ttsz + tw]
        read_bank(s1b, "vector", lambda e, o=m_row, i=s1b.ps[0:1, 0:tw]:
                  e.tensor_scalar_mul(out=o, in0=i, scalar1=1.0 / DM),
                  waits=list(ip_waits) + [buf_rd.get("lnrow_pe")])
        vop(lambda e, o=var_row, i=m_row: e.tensor_mul(out=o, in0=i, in1=i),
            waits=[ve.w()])
        read_bank(s2b, "vector", lambda e, o=var_row, i=s2b.ps[0:1, 0:tw],
                  i1=var_row: e.scalar_tensor_tensor(
                      out=o, in0=i, scalar=1.0 / DM, in1=i1,
                      op0=ALU.mult, op1=ALU.subtract), waits=[ve.w()])
        wv = ve.w()
        w1 = sop(lambda e, o=rstd_row, i=var_row: e.activation(
            out=o, in_=i, func=AF.Ln, bias=ln_eps), waits=[wv, xsq_ready])
        w2 = sop(lambda e, o=rstd_row: e.activation(
            out=o, in_=o, func=AF.Exp, scale=-0.5), waits=[w1])
        wm = vop(lambda e, o=mrs_row, i=m_row, i1=rstd_row: e.tensor_mul(
            out=o, in0=i, in1=i1), waits=[w2, ve.w()])
        rb, mb = banks[4], banks[5]
        mms(rb, [(onesr[0:1, :], rstd_row)], tw, waits=[wm])
        mms(mb, [(onesr[0:1, :], mrs_row)], tw, waits=[wm])
        buf_rd["lnrow_pe"] = pe.w()
        for db in range(8):
            read_bank(rb, "vector", lambda e,
                      o=x_ap[:, db * tw:(db + 1) * tw],
                      i=x_ap[:, db * tw:(db + 1) * tw],
                      p=rb.ps[:, 0:tw]: e.tensor_mul(out=o, in0=i, in1=p))
        for db in range(8):
            read_bank(mb, "vector", lambda e,
                      o=xout_bf[:, db * tw:(db + 1) * tw],
                      i=x_ap[:, db * tw:(db + 1) * tw],
                      p=mb.ps[:, 0:tw]: e.tensor_sub(out=o, in0=i, in1=p))
        return ve.w(), pe.w()

    def a1_tile(t0, tw, full):
        ntb = tw // 128
        for tb in range(ntb):
            par = tb % 2
            sbuf = x_tok_sb[par]
            xsem = (st, po)[par]
            S.add("sync", lambda e, d=sbuf,
                  s2=x_tok[t0 + tb * 128: t0 + (tb + 1) * 128, :]:
                  e.dma_start(out=d, in_=s2),
                  waits=[buf_rd.get(("xtok", par)), buf_rd.get("ld0")],
                  inc=xsem, amt=16)
            w = xsem.w()
            for db in range(8):
                bank = banks[db % 2]
                transpose_into(bank, sbuf[:, db * 128:(db + 1) * 128], ident,
                               waits=[w])
                if db == 7:
                    buf_rd[("xtok", par)] = pe.w()
                read_bank(bank, "scalar", lambda e,
                          o=x_fm[:, db * tw + tb * 128: db * tw + (tb + 1) * 128],
                          p=bank.ps[0:128, 0:128]: e.copy(out=o, in_=p),
                          waits=([buf_rd.get("xfmbuf_ve"), buf_rd.get("xfmbuf_st")]
                                 if (tb == 0 and db == 0) else ()))
        xfm_ready = se.w()
        xfm_spilled = None
        if full:
            for a in range(8):
                xfm_spilled = dma_out(sp_xfm[a, :, t0 - HALO: t0 - HALO + tw],
                                      x_fm[:, a * tw:(a + 1) * tw],
                                      waits=[xfm_ready] if a == 0 else (),
                                      sem=st1)
        rw = [xfm_ready] + ([xfm_spilled] if xfm_spilled else [])
        xln_ready, ln_pe = emit_ln(
            x_fm, tw, xbf, lnrow, TT_A, rw,
            [buf_rd.get("szxq")], [buf_rd.get("lnrow_st")])
        buf_rd["lnrow_st"] = None
        nob = 32 if full else 16
        for ob in range(nob):
            bank = banks[ob % 2]
            mms(bank, [(w_ip_bf[:, kc * 2 * DI + ob * 128:
                                kc * 2 * DI + (ob + 1) * 128],
                        xbf[:, kc * tw:(kc + 1) * tw]) for kc in range(8)],
                tw, waits=[xln_ready, wipr])
            if ob < 16:
                read_bank(bank, "scalar", lambda e,
                          o=xp_sb[:, ob * (TT_A + 3) + 3:
                                  ob * (TT_A + 3) + 3 + tw],
                          p=bank.ps[:, 0:tw]: e.copy(out=o, in_=p),
                          waits=[buf_rd.get("xp")] if ob == 0 else ())
            else:
                read_bank(bank, "scalar", lambda e,
                          o=szxq[:, (ob - 16) * tw:(ob - 15) * tw],
                          p=bank.ps[:, 0:tw]: e.activation(
                              out=o, in_=p, func=AF.Silu),
                          waits=[buf_rd.get("szxq")] if ob == 16 else ())
        buf_rd["a1xbf_pe"] = pe.w()
        xp_ready = se.w()
        sz_ready = se.w()
        if full:
            sz_done = None
            for a in range(16):
                sz_done = dma_out(sp_sz[a, :, t0 - HALO: t0 - HALO + tw],
                                  szxq[:, a * tw:(a + 1) * tw],
                                  waits=[sz_ready] if a == 0 else (), sem=stsz)
            buf_rd["szxq"] = sz_done
            buf_rd["lnrow_st"] = sz_done
            wxs0 = buf_rd.get("xs")
            for g in range(16):
                base = g * (TT_A + 3)
                vop(lambda e, o=xs_f[:, g * tw:(g + 1) * tw],
                    i=xp_sb[:, base:base + tw],
                    s=convw[:, g * KCONV:g * KCONV + 1]: e.tensor_scalar(
                        out=o, in0=i, scalar1=s, scalar2=None, op0=ALU.mult),
                    waits=[xp_ready, wxs0] if g == 0 else ())
                for k in range(1, KCONV):
                    vop(lambda e, o=xs_f[:, g * tw:(g + 1) * tw],
                        i=xp_sb[:, base + k:base + k + tw],
                        s=convw[:, g * KCONV + k:g * KCONV + k + 1]:
                        e.scalar_tensor_tensor(out=o, in0=i, scalar=s, in1=o,
                                               op0=ALU.mult, op1=ALU.add))
            conv_done = ve.w()
            for g in range(16):
                sop(lambda e, o=xs_f[:, g * tw:(g + 1) * tw],
                    b=convb[:, g:g + 1]: e.activation(
                        out=o, in_=o, func=AF.Silu, bias=b),
                    waits=[conv_done] if g == 0 else ())
            xs_ready = se.w()
            xs_bf_sp = x_fm.bitcast(BF16)[:, 0:16 * tw]
            for g in range(16):
                vop(lambda e, o=xs_bf_sp[:, g * tw:(g + 1) * tw],
                    i=xs_f[:, g * tw:(g + 1) * tw]: e.tensor_copy(out=o, in_=i),
                    waits=[xs_ready] if g == 0 else ())
            wcast = ve.w()
            buf_rd["xs"] = wcast
            buf_rd["xfmbuf_ve"] = wcast
            wxsp = None
            for a in range(16):
                wxsp = dma_out(sp_xs[a, :, t0 - HALO: t0 - HALO + tw],
                               xs_bf_sp[:, a * tw:(a + 1) * tw],
                               waits=[wcast] if a == 0 else (), sem=stxs)
            buf_rd["xfmbuf_st"] = wxsp
        for g in range(16):
            b = g * (TT_A + 3)
            vop(lambda e, o=xp_sb[:, b:b + 3], i=xp_sb[:, b + tw:b + tw + 3]:
                e.tensor_copy(out=o, in_=i),
                waits=[xp_ready] if g == 0 else ())
        buf_rd["xp"] = ve.w()

    a1_tile(0, HALO, full=False)
    for ti in range(NT_A):
        a1_tile(HALO + ti * TT_A, TT_A, full=True)
    S.fence([ve.w(), se.w(), pe.w(), st.w()])
    A.pop()

    A.push()
    w_dt_bf = A.alloc(16 * DI, BF16)
    w_bc_bf = A.alloc(16 * 2 * DS, BF16)
    identb = A.alloc(128, BF16)
    sel16b = A.alloc(16 * 128, BF16)
    vop(lambda e: e.tensor_copy(out=identb, in_=ident))
    vop(lambda e: e.tensor_copy(out=sel16b[0:16, :], in_=sel16[0:16, :]))
    y_acc = A.alloc(16 * TT_A, F32)
    xs_bf = y_acc.bitcast(BF16)[:, 0:16 * TT_A]
    dt_sb = A.alloc(16 * TT_A, F32)
    dx_sb = A.alloc(16 * TT_A, BF16)
    e1_one = A.alloc(TT_A, F32)
    e1_sb = [e1_one, e1_one]
    a_t = [A.alloc(TT_A, F32) for _ in range(4)]
    _bt = A.alloc(TT_A, BF16)
    b_t = [_bt, _bt]
    h_t = A.alloc(TT_A, BF16)
    tmp_t = [A.alloc(TT_A, BF16) for _ in range(4)]
    bp_sb = A.alloc(TT_A, BF16)
    cp_sb = A.alloc(TT_A, BF16)
    bb_sb = [A.alloc(TT_A, BF16), A.alloc(TT_A, BF16)]
    cb_sbuf = [A.alloc(TT_A, BF16), A.alloc(TT_A, BF16)]
    g_sb = dx_sb.bitcast(F32)[:, 0:8 * 16 * DS]

    wstage = dt_sb
    load_cast(w_dt, DI, DI, w_dt_bf, wstage[:, 0:DI], chunk_cols=1024)
    wdtr = load_cast(w_bc, DI, 2 * DS, w_bc_bf, wstage[:, 0:2 * DS])

    a_rd = [None] * 4
    state_w = {}

    def a2_tile(xs_src, tw, do_y, first_scan_wait=None):
        wx = None
        for a in range(16):
            wx = dma_in(xs_bf[:, a * tw:(a + 1) * tw], xs_src(a),
                        waits=[buf_rd.get("yspill"), buf_rd.get("a2dtx")]
                        if a == 0 else ())
        for ob in range(16):
            bank = banks[ob % 2]
            mms(bank, [(w_dt_bf[:, kc * DI + ob * 128: kc * DI + (ob + 1) * 128],
                        xs_bf[:, kc * tw:(kc + 1) * tw]) for kc in range(16)],
                tw, waits=[wx, wdtr])
            e1 = e1_sb[ob % 2]
            we = read_bank(bank, "scalar", lambda e, o=e1[:, 0:tw],
                           p=bank.ps[:, 0:tw], b=dtbc[:, ob:ob + 1]:
                           e.activation(out=o, in_=p, func=AF.Exp, bias=b))
            sop(lambda e, o=dt_sb[:, ob * tw:(ob + 1) * tw], i=e1[:, 0:tw]:
                e.activation(out=o, in_=i, func=AF.Ln, bias=1.0), waits=[we])
        dt_ready = se.w()
        bb_, cb_ = banks[2], banks[3]
        mms(bb_, [(w_bc_bf[:, kc * 2 * DS: kc * 2 * DS + DS],
                   xs_bf[:, kc * tw:(kc + 1) * tw]) for kc in range(16)],
            tw, out_rows=16, waits=[wx, wdtr])
        mms(cb_, [(w_bc_bf[:, kc * 2 * DS + DS: (kc + 1) * 2 * DS],
                   xs_bf[:, kc * tw:(kc + 1) * tw]) for kc in range(16)],
            tw, out_rows=16, waits=[wx])
        read_bank(bb_, "vector", lambda e, o=bp_sb[0:16, 0:tw],
                  p=bb_.ps[0:16, 0:tw]: e.tensor_copy(out=o, in_=p),
                  waits=[buf_rd.get("a2bp"), buf_rd.get("a2bp_pe")])
        read_bank(cb_, "vector", lambda e, o=cp_sb[0:16, 0:tw],
                  p=cb_.ps[0:16, 0:tw]: e.tensor_copy(out=o, in_=p))
        bp_ready = ve.w()
        for g in range(16):
            vop(lambda e, o=dx_sb[:, g * tw:(g + 1) * tw],
                i=dt_sb[:, g * tw:(g + 1) * tw],
                i1=xs_bf[:, g * tw:(g + 1) * tw]: e.tensor_mul(
                    out=o, in0=i, in1=i1),
                waits=[dt_ready] if g == 0 else ())
        dx_ready = ve.w()
        passes = ([range(0, 4), range(4, 8), range(8, 12), range(12, 16)]
                  if do_y else [range(16)])
        spill_waits = []

        def emit_sel(n, par):
            bbk, cbk = banks[4 + par], banks[6 + par]
            mms(bbk, [(sel16b[0:16, n * 128:(n + 1) * 128],
                       bp_sb[0:16, 0:tw])], tw, waits=[bp_ready])
            if do_y:
                mms(cbk, [(sel16b[0:16, n * 128:(n + 1) * 128],
                           cp_sb[0:16, 0:tw])], tw, waits=[bp_ready])

        def emit_evac(n, par):
            bbk, cbk = banks[4 + par], banks[6 + par]
            wbe = read_bank(bbk, "vector", lambda e, o=bb_sb[par][:, 0:tw],
                            p=bbk.ps[:, 0:tw]: e.tensor_copy(out=o, in_=p))
            wce = None
            if do_y:
                wce = read_bank(cbk, "vector", lambda e, o=cb_sbuf[par][:, 0:tw],
                                p=cbk.ps[:, 0:tw]: e.tensor_copy(out=o, in_=p))
            return wbe, wce

        emit_sel(0, 0)
        for pi, gl in enumerate(passes):
            for n in range(DS):
                par = n % 2
                bbs = bb_sb[par]
                cbs = cb_sbuf[par]
                if n + 1 < DS:
                    emit_sel(n + 1, (n + 1) % 2)
                elif pi + 1 < len(passes):
                    emit_sel(0, 0)
                wbe, wce = emit_evac(n, par)
                for gi, g in enumerate(gl):
                    ab = a_t[g % 4]
                    wa = sop(lambda e, o=ab[:, 0:tw],
                             i=dt_sb[:, g * tw:(g + 1) * tw],
                             s=A_sb[:, g * DS + n: g * DS + n + 1]: e.activation(
                                 out=o, in_=i, func=AF.Exp, scale=s),
                             waits=[w for w in (a_rd[g % 4],) if w])
                    bt = b_t[g % 2]
                    vop(lambda e, o=bt[:, 0:tw],
                        i=dx_sb[:, g * tw:(g + 1) * tw],
                        i1=bbs[:, 0:tw]: e.tensor_tensor(
                            out=o, in0=i, in1=i1, op=ALU.mult),
                        waits=[wbe] if g == gl[0] else ())
                    col = state[:, g * DS + n: g * DS + n + 1]
                    extra = []
                    if first_scan_wait is not None and n == 0 and gi == 0:
                        extra.append(first_scan_wait)
                    vop(lambda e, d0=ab[:, 0:tw], d1=bt[:, 0:tw], o=h_t[:, 0:tw],
                        c=col: e.tensor_tensor_scan(
                            out=o, data0=d0, data1=d1, initial=c,
                            op0=ALU.mult, op1=ALU.add),
                        waits=[wa, state_w.get((n, g))] + extra)
                    a_rd[g % 4] = ve.w()
                    if do_y:
                        ybank = banks[gi]
                        tp_ = tmp_t[g % 4]
                        wcm = vop(lambda e, o=tp_[:, 0:tw], i=h_t[:, 0:tw],
                                  i1=cbs[:, 0:tw]: e.tensor_mul(
                                      out=o, in0=i, in1=i1),
                                  waits=[wce, buf_rd.get(("tmprd", g % 4))])
                        wmm = list(ybank.rd) if n == 0 else []
                        S.add("tensor",
                              (lambda e, o=ybank.ps[:, 0:tw], l=identb,
                               r=tp_[:, 0:tw], st_=(n == 0), sp=(n == DS - 1):
                               e.matmul(out=o, lhsT=l, rhs=r, start=st_,
                                        stop=sp, skip_group_check=True)),
                              waits=[wcm] + wmm, inc=pe)
                        buf_rd[("tmprd", g % 4)] = pe.w()
                        if n == 0:
                            ybank.rd = []
                        if n == DS - 1:
                            ybank.wr = pe.w()
                    state_w[(n, g)] = sop(
                        lambda e, c=col, i=h_t[:, tw - 1:tw]: e.copy(
                            out=c, in_=i), waits=[ve.w()])

            if do_y:
                for gi, g in enumerate(gl):
                    ybank = banks[gi]
                    read_bank(ybank, "vector", lambda e,
                              o=y_acc[:, g * tw:(g + 1) * tw],
                              p=ybank.ps[:, 0:tw]: e.tensor_copy(out=o, in_=p),
                              waits=[buf_rd.get("ysp_all")]
                              if (gi == 0 and gl[0] == 0) else ())
                spill_waits.append(ve.w())
        buf_rd["a2bp"] = ve.w()
        buf_rd["a2bp_pe"] = pe.w()
        return spill_waits

    vop(lambda e: e.memset(state, 0.0))
    ws0 = ve.w()
    a2_tile(lambda a: sp_xs[a, :, TCORE - HALO:TCORE], HALO, do_y=False,
            first_scan_wait=ws0)
    tail_done = ve.w()
    w1 = dma_out(cc_in, state, waits=[tail_done], sem=sty)
    S.add("gpsimd", lambda e: e.collective_compute(
        "AllGather", ALU.bypass, replica_groups=[list(range(NCORES))],
        ins=[cc_in], outs=[cc_out]), waits=[w1], inc=ccs, amt=1)
    S.add("sync", lambda e, d=g_sb.rearrange("p (r c) -> p r c", r=NCORES),
          s2=cc_out.rearrange("(r p) c -> p r c", p=128):
          e.dma_start(out=d, in_=s2), waits=[ccs.w()], inc=ldg, amt=16)
    wg = ldg.w()
    vop(lambda e: e.memset(state, 0.0), waits=[tail_done, w1])
    for j in range(1, NCORES):
        vop(lambda e, i=g_sb[:, (j - 1) * 256: j * 256], s=sel8[:, j:j + 1]:
            e.scalar_tensor_tensor(out=state, in0=i, scalar=s, in1=state,
                                   op0=ALU.mult, op1=ALU.add),
            waits=[wg, ve.w()] if j == 1 else [ve.w()])
    hinit_ready = ve.w()

    for ti in range(NT_A):
        t0 = ti * TT_A
        sw = a2_tile(lambda a, t0=t0: sp_xs[a, :, t0:t0 + TT_A], TT_A, do_y=True,
                     first_scan_wait=hinit_ready if ti == 0 else None)
        glists = [range(0, 4), range(4, 8), range(8, 12), range(12, 16)]
        wy = None
        for gl, w in zip(glists, sw):
            for a in gl:
                wy = dma_out(sp_y[a, :, t0:t0 + TT_A],
                             y_acc[:, a * TT_A:(a + 1) * TT_A],
                             waits=[w] if a == gl[0] else (), sem=sty)
        buf_rd["yspill"] = wy
        buf_rd["ysp_all"] = wy
        buf_rd["a2dtx"] = ve.w()
    S.fence([ve.w(), se.w(), pe.w(), st.w(), ld.w(), po.w()])
    A.pop()

    xln2bf = A.alloc(8 * TCORE, BF16)
    comb16 = A.alloc(TCORE, F32)
    A.push()
    w_out_bf = A.alloc(16 * DM, BF16)
    wg_sb = A.alloc(8 * NE, F32)
    stage4b = A.alloc(1024, F32)
    y_f = A.alloc(16 * TT_C1, F32)
    xs_c = A.alloc(16 * TT_C1, BF16)
    sz_c = A.alloc(16 * TT_C1, BF16)
    ybf = A.alloc(16 * TT_C1, BF16)
    xfm_c = A.alloc(8 * TT_C1, F32)
    x2_c = A.alloc(8 * TT_C1, F32)
    xl2f = A.alloc(8 * TT_C1, F32)
    lnrow2 = A.alloc(4 * TT_C1, F32)
    srow = A.alloc(TT_C1, F32)
    sT = A.alloc(2 * NE, F32)
    gte = A.alloc(512, F32)
    ntb1 = TT_C1 // 128
    B2 = ntb1

    def vopE(fn, waits=()):
        return vop(fn, waits=list(waits) + [ve.w()])

    load_cast(w_out, DI, DM, w_out_bf, stage4b, chunk_cols=1024)

    for kc in range(8):
        S.add("sync", lambda e, d=wg_sb[:, kc * NE:(kc + 1) * NE],
              s2=w_gate[kc * 128:(kc + 1) * 128, :]: e.dma_start(out=d, in_=s2),
              inc=ldg, amt=16)
    wgw = ldg.w()

    for ti in range(NT_C1):
        t0 = ti * TT_C1
        tw = TT_C1
        prev = buf_rd.get("c1prev") or []
        wy = None
        for a in range(16):
            dma_in(y_f[:, a * tw:(a + 1) * tw], sp_y[a, :, t0:t0 + tw],
                   waits=list(prev) if a == 0 else ())
            dma_in(xs_c[:, a * tw:(a + 1) * tw], sp_xs[a, :, t0:t0 + tw])
            wy = dma_in(sz_c[:, a * tw:(a + 1) * tw], sp_sz[a, :, t0:t0 + tw])
        wxf = None
        for a in range(8):
            wxf = dma_in(xfm_c[:, a * tw:(a + 1) * tw], sp_xfm[a, :, t0:t0 + tw])
        for g in range(16):
            vop(lambda e, o=y_f[:, g * tw:(g + 1) * tw],
                i=xs_c[:, g * tw:(g + 1) * tw], s=dpc[:, g:g + 1]:
                e.scalar_tensor_tensor(out=o, in0=i, scalar=s, in1=o,
                                       op0=ALU.mult, op1=ALU.add),
                waits=[wxf] if g == 0 else ())
        for g in range(16):
            vop(lambda e, o=ybf[:, g * tw:(g + 1) * tw],
                i=y_f[:, g * tw:(g + 1) * tw],
                i1=sz_c[:, g * tw:(g + 1) * tw]: e.tensor_mul(
                    out=o, in0=i, in1=i1))
        ybf_ready = ve.w()
        for db in range(8):
            bank = banks[db % 2]
            mms(bank, [(w_out_bf[:, kc * DM + db * 128: kc * DM + (db + 1) * 128],
                        ybf[:, kc * tw:(kc + 1) * tw]) for kc in range(16)],
                tw, waits=[ybf_ready])
            read_bank(bank, "vector", lambda e, o=x2_c[:, db * tw:(db + 1) * tw],
                      i=xfm_c[:, db * tw:(db + 1) * tw], p=bank.ps[:, 0:tw]:
                      e.tensor_add(out=o, in0=i, in1=p),
                      waits=[wxf, buf_rd.get("c1x2sp")] if db == 0 else ())
        x2_ready = ve.w()
        wsp = None
        for a in range(8):
            wsp = dma_out(sp_x2[a, :, t0:t0 + tw], x2_c[:, a * tw:(a + 1) * tw],
                          waits=[x2_ready] if a == 0 else (), sem=stx2)
        buf_rd["c1x2sp"] = wsp
        xsq2 = ybf
        xb2 = xs_c
        for db in range(8):
            sop(lambda e, o=xsq2[:, db * tw:(db + 1) * tw],
                i=x2_c[:, db * tw:(db + 1) * tw]: e.activation(
                    out=o, in_=i, func=AF.Square),
                waits=[x2_ready] if db == 0 else ())
        xsq_ready = se.w()
        for db in range(8):
            vop(lambda e, o=xb2[:, db * tw:(db + 1) * tw],
                i=x2_c[:, db * tw:(db + 1) * tw]: e.tensor_copy(out=o, in_=i),
                waits=[x2_ready] if db == 0 else ())
        xb_ready = ve.w()
        s1b, s2b = banks[2], banks[3]
        mms(s1b, [(onescb, xb2[:, db * tw:(db + 1) * tw]) for db in range(8)],
            tw, out_rows=1, waits=[xb_ready])
        mms(s2b, [(onescb, xsq2[:, db * tw:(db + 1) * tw]) for db in range(8)],
            tw, out_rows=1, waits=[xsq_ready])
        m_row = lnrow2[0:1, 0:tw]
        var_row = lnrow2[0:1, TT_C1:TT_C1 + tw]
        rstd_row = lnrow2[0:1, 2 * TT_C1:2 * TT_C1 + tw]
        mrs_row = lnrow2[0:1, 3 * TT_C1:3 * TT_C1 + tw]
        buf_rd["c1prev"] = [ve.w(), se.w(), pe.w()]
        read_bank(s1b, "vector", lambda e, o=m_row, p=s1b.ps[0:1, 0:tw]:
                  e.tensor_scalar_mul(out=o, in0=p, scalar1=1.0 / DM))
        vop(lambda e, o=var_row, i=m_row: e.tensor_mul(out=o, in0=i, in1=i),
            waits=[ve.w()])
        read_bank(s2b, "vector", lambda e, o=var_row, p=s2b.ps[0:1, 0:tw],
                  i1=var_row: e.scalar_tensor_tensor(
                      out=o, in0=p, scalar=1.0 / DM, in1=i1,
                      op0=ALU.mult, op1=ALU.subtract), waits=[ve.w()])
        wv = ve.w()
        wl1 = sop(lambda e, o=rstd_row, i=var_row: e.activation(
            out=o, in_=i, func=AF.Ln, bias=ln_eps), waits=[wv, xsq_ready])
        wl2 = sop(lambda e, o=rstd_row: e.activation(
            out=o, in_=o, func=AF.Exp, scale=-0.5), waits=[wl1])
        wm = vop(lambda e, o=mrs_row, i=m_row, i1=rstd_row: e.tensor_mul(
            out=o, in0=i, in1=i1), waits=[wl2, ve.w()])
        rb, mb = banks[4], banks[5]
        mms(rb, [(onesr[0:1, :], rstd_row)], tw, waits=[wm])
        mms(mb, [(onesr[0:1, :], mrs_row)], tw, waits=[wm])
        for db in range(8):
            read_bank(rb, "vector", lambda e, o=xl2f[:, db * tw:(db + 1) * tw],
                      i=x2_c[:, db * tw:(db + 1) * tw], p=rb.ps[:, 0:tw]:
                      e.tensor_mul(out=o, in0=i, in1=p))
        for db in range(8):
            read_bank(mb, "vector", lambda e, o=xl2f[:, db * tw:(db + 1) * tw],
                      p=mb.ps[:, 0:tw]: e.tensor_sub(out=o, in0=o, in1=p))
        xl2_ready = ve.w()
        for db in range(8):
            vop(lambda e, o=xln2bf[:, db * TCORE + t0: db * TCORE + t0 + tw],
                i=xl2f[:, db * tw:(db + 1) * tw]: e.tensor_copy(out=o, in_=i))
        gb = banks[6]
        mms(gb, [(wg_sb[:, kc * NE:(kc + 1) * NE], xl2f[:, kc * tw:(kc + 1) * tw])
                 for kc in range(8)], tw, out_rows=16, waits=[xl2_ready, wgw])
        wsg = read_bank(gb, "scalar", lambda e, o=srow[0:16, 0:tw],
                        p=gb.ps[0:16, 0:tw]: e.activation(
                            out=o, in_=p, func=AF.Exp, scale=-1.0))
        vop(lambda e, o=srow[0:16, 0:tw]: e.tensor_scalar_add(
            out=o, in0=o, scalar1=1.0), waits=[wsg])
        wsg = vop(lambda e, o=srow[0:16, 0:tw]: e.reciprocal(out=o, in_=o),
                  waits=[ve.w()])
        for tb in range(ntb1):
            bank = banks[7]
            transpose_into(bank, srow[0:16, tb * 128:(tb + 1) * 128],
                           ident[0:16, 0:16], waits=[wsg])
            read_bank(bank, "vector", lambda e, o=sT[:, tb * NE:(tb + 1) * NE],
                      p=bank.ps[0:128, 0:16]: e.tensor_copy(out=o, in_=p))
        wst = ve.w()
        sT3 = sT.rearrange("p (b e) -> p b e", b=B2)
        v4 = sT.rearrange("p (b g j) -> p b g j", b=B2, g=4)
        npr = B2 * 4 * 2
        pmx = gte[:, 0:npr].rearrange("p (b g i) -> p b g i", b=B2, g=4)
        pmn = gte[:, npr:2 * npr].rearrange("p (b g i) -> p b g i", b=B2, g=4)
        gs = gte[:, 2 * npr:2 * npr + B2 * 4].rearrange("p (b g) -> p b g", b=B2)
        base = 2 * npr + B2 * 4
        t_a = gte[:, base:base + B2]
        t_b = gte[:, base + B2:base + 2 * B2]
        g2 = gte[:, base + 2 * B2:base + 3 * B2]
        rmax = gte[:, base + 3 * B2:base + 4 * B2]
        base += 4 * B2
        alw4 = gte[:, base:base + 4 * B2].rearrange("p (b g) -> p b g", b=B2)
        base += 4 * B2
        mskF = gte[:, base:base + 16 * B2]
        msk = mskF.rearrange("p (b e) -> p b e", b=B2)
        msk4 = mskF.rearrange("p (b g j) -> p b g j", b=B2, g=4)
        base += 16 * B2
        cur = gte[:, base:base + 16 * B2].rearrange("p (b e) -> p b e", b=B2)
        base += 16 * B2
        eqm = gte[:, base:base + 16 * B2].rearrange("p (b e) -> p b e", b=B2)
        base += 16 * B2
        wselF = gte[:, base:base + 16 * B2]
        wsel = wselF.rearrange("p (b e) -> p b e", b=B2)
        base += 16 * B2
        wsum = gte[:, base:base + B2]
        BIG = 1.0e30
        vopE(lambda e: e.tensor_max(out=pmx[:, :, :, 0], in0=v4[:, :, :, 0],
                                   in1=v4[:, :, :, 1]), waits=[wst])
        vopE(lambda e: e.tensor_max(out=pmx[:, :, :, 1], in0=v4[:, :, :, 2],
                                   in1=v4[:, :, :, 3]))
        vopE(lambda e: e.tensor_tensor(out=pmn[:, :, :, 0], in0=v4[:, :, :, 0],
                                      in1=v4[:, :, :, 1], op=ALU.min))
        vopE(lambda e: e.tensor_tensor(out=pmn[:, :, :, 1], in0=v4[:, :, :, 2],
                                      in1=v4[:, :, :, 3], op=ALU.min))
        vopE(lambda e: e.tensor_max(out=gs[:, :, :], in0=pmx[:, :, :, 0],
                                   in1=pmx[:, :, :, 1]))
        vopE(lambda e: e.tensor_max(out=pmn[:, :, :, 0], in0=pmn[:, :, :, 0],
                                   in1=pmn[:, :, :, 1]))
        vopE(lambda e: e.tensor_tensor(out=pmx[:, :, :, 0], in0=pmx[:, :, :, 0],
                                      in1=pmx[:, :, :, 1], op=ALU.min))
        vopE(lambda e: e.tensor_max(out=pmx[:, :, :, 0], in0=pmx[:, :, :, 0],
                                   in1=pmn[:, :, :, 0]))
        vopE(lambda e: e.tensor_add(out=gs[:, :, :], in0=gs[:, :, :],
                                   in1=pmx[:, :, :, 0]))
        vopE(lambda e: e.tensor_max(out=t_a, in0=gs[:, :, 0], in1=gs[:, :, 1]))
        vopE(lambda e: e.tensor_max(out=t_b, in0=gs[:, :, 2], in1=gs[:, :, 3]))
        vopE(lambda e: e.tensor_tensor(out=g2, in0=t_a, in1=t_b, op=ALU.min))
        vopE(lambda e: e.tensor_tensor(out=t_a, in0=gs[:, :, 0], in1=gs[:, :, 1],
                                      op=ALU.min))
        vopE(lambda e: e.tensor_tensor(out=t_b, in0=gs[:, :, 2], in1=gs[:, :, 3],
                                      op=ALU.min))
        vopE(lambda e: e.tensor_max(out=t_a, in0=t_a, in1=t_b))
        vopE(lambda e: e.tensor_max(out=g2, in0=g2, in1=t_a))
        g2b = g2.rearrange("p (b o) -> p b o", o=1).broadcast_to([128, B2, 4])
        vopE(lambda e: e.tensor_tensor(out=alw4[:, :, :], in0=gs[:, :, :],
                                      in1=g2b, op=ALU.is_ge))
        vopE(lambda e: e.tensor_scalar_add(out=alw4[:, :, :], in0=alw4[:, :, :],
                                          scalar1=-1.0))
        alwE = alw4.rearrange("p b (g o) -> p b g o", o=1).broadcast_to(
            [128, B2, 4, 4])
        vopE(lambda e: e.scalar_tensor_tensor(
            out=msk4[:, :, :, :], in0=alwE, scalar=BIG, in1=v4[:, :, :, :],
            op0=ALU.mult, op1=ALU.add))
        src = msk
        for it in range(4):
            vopE(lambda e, s=src: e.tensor_reduce(out=rmax, in_=s[:, :, :],
                                                 axis=AX.X, op=ALU.max))
            if it < 3:
                rmb = rmax.rearrange("p (b o) -> p b o", o=1).broadcast_to(
                    [128, B2, 16])
                vopE(lambda e, s=src, rb_=rmb: e.tensor_tensor(
                    out=eqm[:, :, :], in0=s[:, :, :], in1=rb_, op=ALU.is_ge))
                vopE(lambda e, s=src: e.scalar_tensor_tensor(
                    out=cur[:, :, :], in0=eqm[:, :, :], scalar=-BIG,
                    in1=s[:, :, :], op0=ALU.mult, op1=ALU.add))
                src = cur
        rmb = rmax.rearrange("p (b o) -> p b o", o=1).broadcast_to([128, B2, 16])
        vopE(lambda e: e.tensor_tensor(out=eqm[:, :, :], in0=msk[:, :, :],
                                      in1=rmb, op=ALU.is_ge))
        vopE(lambda e: e.tensor_mul(out=wsel[:, :, :], in0=sT3[:, :, :],
                                   in1=eqm[:, :, :]))
        vopE(lambda e: e.tensor_reduce(out=wsum, in_=wsel[:, :, :], axis=AX.X,
                                      op=ALU.add))
        vopE(lambda e: e.tensor_scalar_add(out=wsum, in0=wsum, scalar1=1e-6))
        wrc = vopE(lambda e: e.reciprocal(out=wsum, in_=wsum))
        wsb = wsum.rearrange("p (b o) -> p b o", o=1).broadcast_to([128, B2, 16])
        wcm = vopE(lambda e: e.tensor_tensor(out=wsel[:, :, :], in0=wsel[:, :, :],
                                            in1=wsb, op=ALU.mult), waits=[wrc])
        for tb in range(ntb1):
            bank = banks[7]
            transpose_into(bank, wselF[:, tb * 16:(tb + 1) * 16], ident,
                           waits=[wcm])
            read_bank(bank, "vector", lambda e,
                      o=comb16[0:16, t0 + tb * 128: t0 + (tb + 1) * 128],
                      p=bank.ps[0:16, 0:128]: e.tensor_copy(out=o, in_=p))
    S.fence([ve.w(), se.w(), pe.w(), st.w(), ld.w()])
    A.pop()

    routed = A.alloc(8 * TCORE, F32)
    A.push()
    wexp = [dict(w1=A.alloc(8 * FF, BF16), w3=A.alloc(8 * FF, BF16),
                 w2=A.alloc(4 * DM, BF16)) for _ in range(2)]
    gbf = A.alloc(4 * TT_C2, BF16)
    gubf2 = [A.alloc(4 * TT_C2, BF16), A.alloc(4 * TT_C2, BF16)]
    ubf = A.alloc(TT_C2, BF16)
    cb_sb = A.alloc(TT_C2, F32)

    stg_c2 = [A.alloc(512, F32) for _ in range(4)]
    stg_state = {"cast": [None] * 4}
    stg_sems = [lde0, lde1, ldpc, stpc]

    def load_expert(e_idx, slot, waits):
        sem = (lde0, lde1)[slot]
        ws = list(waits)
        jobs = []
        for part, src, rows, cols in ((0, w1e[e_idx], DM, FF),
                                      (1, w3e[e_idx], DM, FF),
                                      (2, w2e[e_idx], FF, DM)):
            dst = wexp[slot]["w1" if part == 0 else "w3" if part == 1 else "w2"]
            for kc in range(rows // 128):
                for c0 in range(0, cols, 512):
                    jobs.append((src[kc * 128:(kc + 1) * 128, c0:c0 + 512],
                                 dst[:, kc * cols + c0:kc * cols + c0 + 512]))
        last = None
        for j, (srcj, dstj) in enumerate(jobs):
            par = j % 4
            ssem = stg_sems[par]
            S.add("sync", lambda e, d=stg_c2[par], s2=srcj: e.dma_start(
                out=d, in_=s2),
                waits=(ws if j == 0 else []) +
                [w for w in (stg_state["cast"][par],) if w],
                inc=ssem, amt=16)
            wdma = ssem.w()
            last = vop(lambda e, o=dstj, i=stg_c2[par]: e.tensor_copy(
                out=o, in_=i), waits=[wdma])
            stg_state["cast"][par] = last
        return last

    wexp_ready = [None, None]
    wexp_free = [None, None]
    wexp_ready[0] = load_expert(0, 0, ())
    pending = None
    items = [(e, ti) for e in range(NE) for ti in range(NT_C2)]
    for it_i, (e_i, ti) in enumerate(items):
        slot = e_i % 2
        gpar = it_i % 2
        gubf = gubf2[gpar]
        wr = wexp_ready[slot]
        t0 = ti * TT_C2
        tw = TT_C2
        cbk = banks[2]
        mms(cbk, [(sel16[0:16, e_i * 128:(e_i + 1) * 128],
                   comb16[0:16, t0:t0 + tw])], tw)
        wcb = read_bank(cbk, "scalar", lambda e, o=cb_sb[:, 0:tw],
                        p=cbk.ps[:, 0:tw]: e.copy(out=o, in_=p),
                        waits=[buf_rd.get("cbw")])
        for fb in range(4):
            gb_, ub_ = (banks[0], banks[1]) if fb % 2 == 0 else (banks[5], banks[6])
            mms(gb_, [(wexp[slot]["w1"][:, kc * FF + fb * 128:
                                        kc * FF + (fb + 1) * 128],
                       xln2bf[:, kc * TCORE + t0: kc * TCORE + t0 + tw])
                      for kc in range(8)], tw, waits=[wr])
            wsilu = read_bank(gb_, "scalar", lambda e,
                              o=gbf[:, fb * tw:(fb + 1) * tw],
                              p=gb_.ps[:, 0:tw]: e.activation(
                                  out=o, in_=p, func=AF.Silu),
                              waits=[buf_rd.get("c2gbf")] if fb == 0 else ())
            mms(ub_, [(wexp[slot]["w3"][:, kc * FF + fb * 128:
                                        kc * FF + (fb + 1) * 128],
                       xln2bf[:, kc * TCORE + t0: kc * TCORE + t0 + tw])
                      for kc in range(8)], tw, waits=[wr])
            read_bank(ub_, "vector", lambda e, o=ubf[:, 0:tw],
                      p=ub_.ps[:, 0:tw], c=cb_sb[:, 0:tw]: e.tensor_mul(
                          out=o, in0=p, in1=c),
                      waits=[wcb] + ([buf_rd.get(("c2gu", gpar))]
                                     if fb == 0 else []))
            vop(lambda e, o=gubf[:, fb * tw:(fb + 1) * tw],
                i=gbf[:, fb * tw:(fb + 1) * tw], i1=ubf[:, 0:tw]:
                e.tensor_mul(out=o, in0=i, in1=i1), waits=[wsilu])
        buf_rd["cbw"] = ve.w()
        buf_rd["c2gbf"] = ve.w()
        gu_ready = ve.w()

        if pending is not None:
            pending()
        if ti == 0 and e_i + 1 < NE:
            wexp_ready[1 - slot] = load_expert(
                e_i + 1, 1 - slot,
                [wexp_free[1 - slot]] if wexp_free[1 - slot] else ())

        def make_w2(e_i=e_i, ti=ti, slot=slot, gpar=gpar, t0=t0,
                    gu_ready=gu_ready, gubf=gubf):
            def w2():
                tw = TT_C2
                for db in range(8):
                    bank = (banks[3], banks[4], banks[7])[db % 3]
                    mms(bank, [(wexp[slot]["w2"][:, kc * DM + db * 128:
                                                 kc * DM + (db + 1) * 128],
                                gubf[:, kc * tw:(kc + 1) * tw])
                               for kc in range(4)], tw, waits=[gu_ready])
                    if e_i == 0:
                        read_bank(bank, "vector", lambda e,
                                  o=routed[:, db * TCORE + t0:
                                           db * TCORE + t0 + tw],
                                  p=bank.ps[:, 0:tw]: e.tensor_copy(
                                      out=o, in_=p))
                    else:
                        read_bank(bank, "vector", lambda e,
                                  o=routed[:, db * TCORE + t0:
                                           db * TCORE + t0 + tw],
                                  p=bank.ps[:, 0:tw]: e.tensor_add(
                                      out=o, in0=o, in1=p))
                buf_rd[("c2gu", gpar)] = pe.w()
                if ti == NT_C2 - 1:
                    wexp_free[slot] = pe.w()
            return w2

        pending = make_w2()
    pending()
    S.fence([ve.w(), se.w(), pe.w(), st.w(), ld.w()])
    A.pop()

    A.push()
    w1s_bf = A.alloc(8 * FS, BF16)
    w3s_bf = A.alloc(8 * FS, BF16)
    w2s_bf = A.alloc(8 * DM, BF16)
    x2_t = A.alloc(8 * TT_C3, F32)
    stage_c3 = x2_t[:, 0:1024]
    gsbf = A.alloc(TT_C3, BF16)
    gush = A.alloc(8 * TT_C3, BF16)
    _otm = A.alloc(DM, F32)
    out_tm = [_otm, _otm]

    load_cast(w1s, DM, FS, w1s_bf, stage_c3, chunk_cols=1024)
    load_cast(w3s, DM, FS, w3s_bf, stage_c3, chunk_cols=1024)
    wsr = load_cast(w2s, FS, DM, w2s_bf, stage_c3, chunk_cols=1024)

    for ti in range(NT_C3):
        t0 = ti * TT_C3
        tw = TT_C3
        wx2 = None
        for a in range(8):
            wx2 = dma_in(x2_t[:, a * tw:(a + 1) * tw], sp_x2[a, :, t0:t0 + tw],
                         waits=(list(buf_rd.get("c3x2") or []) + [wsr]
                                if a == 0 else ()))
        wguprev = buf_rd.get("c3gu")
        for fb in range(8):
            gb_, ub_ = banks[0], banks[1]
            mms(gb_, [(w1s_bf[:, kc * FS + fb * 128: kc * FS + (fb + 1) * 128],
                       xln2bf[:, kc * TCORE + t0: kc * TCORE + t0 + tw])
                      for kc in range(8)], tw, waits=[wsr])
            wsl = read_bank(gb_, "scalar", lambda e, o=gsbf[:, 0:tw],
                            p=gb_.ps[:, 0:tw]: e.activation(
                                out=o, in_=p, func=AF.Silu),
                            waits=[wguprev] if fb == 0 and wguprev else ())
            mms(ub_, [(w3s_bf[:, kc * FS + fb * 128: kc * FS + (fb + 1) * 128],
                       xln2bf[:, kc * TCORE + t0: kc * TCORE + t0 + tw])
                      for kc in range(8)], tw, waits=[wsr])
            read_bank(ub_, "vector", lambda e, o=gush[:, fb * tw:(fb + 1) * tw],
                      i=gsbf[:, 0:tw], p=ub_.ps[:, 0:tw]: e.tensor_mul(
                          out=o, in0=i, in1=p),
                      waits=[wsl] + ([buf_rd.get("c3gu_pe")]
                                     if fb == 0 else []))
        gu_ready = ve.w()
        buf_rd["c3gu"] = gu_ready
        for db in range(8):
            bank = banks[3 + db % 2]
            mms(bank, [(w2s_bf[:, kc * DM + db * 128: kc * DM + (db + 1) * 128],
                        gush[:, kc * tw:(kc + 1) * tw]) for kc in range(8)],
                tw, waits=[gu_ready])
            vop(lambda e, o=x2_t[:, db * tw:(db + 1) * tw],
                i=routed[:, db * TCORE + t0: db * TCORE + t0 + tw]:
                e.tensor_add(out=o, in0=o, in1=i),
                waits=[wx2] if db == 0 else ())
            read_bank(bank, "vector", lambda e, o=x2_t[:, db * tw:(db + 1) * tw],
                      p=bank.ps[:, 0:tw]: e.tensor_add(out=o, in0=o, in1=p))
        buf_rd["c3gu_pe"] = pe.w()
        o_ready = ve.w()
        for tb in range(tw // 128):
            otm = out_tm[tb % 2]
            for db in range(8):
                bank = banks[6 + db % 2]
                transpose_into(bank,
                               x2_t[:, db * tw + tb * 128: db * tw + (tb + 1) * 128],
                               ident, waits=[o_ready])
                read_bank(bank, "scalar", lambda e,
                          o=otm[:, db * 128:(db + 1) * 128],
                          p=bank.ps[0:128, 0:128]: e.copy(out=o, in_=p),
                          waits=[sto.w()] if db == 0 else ())
            wtm = se.w()
            buf_rd[("otm", tb % 2)] = dma_out(
                out_t[t0 + tb * 128: t0 + (tb + 1) * 128, :], otm, waits=[wtm],
                sem=sto)
        buf_rd["c3x2"] = [ve.w(), pe.w()]
    A.pop()

    if debug:
        dbg_cmb = nc.dram_tensor("dbg_cmb", [16, TCORE], F32,
                                 kind="ExternalOutput").ap()
        dma_out(dbg_cmb, comb16[0:16, :], waits=[ve.w(), se.w(), pe.w()])
        dbg_xl2 = nc.dram_tensor("dbg_xl2", [128, 8 * TCORE], BF16,
                                 kind="ExternalOutput").ap()
        dma_out(dbg_xl2, xln2bf, waits=[ve.w(), se.w(), pe.w()])
        for nm, sp in (("dbg_xs", sp_xs), ("dbg_sz", sp_sz), ("dbg_xfm", sp_xfm),
                       ("dbg_y", sp_y), ("dbg_x2", sp_x2)):
            shp = list(sp.shape)
            dt_ = BF16 if nm in ("dbg_xs", "dbg_sz") else F32
            dbg = nc.dram_tensor(nm, shp, dt_, kind="ExternalOutput").ap()
            dma_out(dbg, sp, waits=[st.w(), ve.w(), se.w(), pe.w()])

    S.add("gpsimd", lambda e: e.nop(),
          waits=[st.w(), ld.w(), pe.w(), ve.w(), se.w(), st1.w(), sty.w(),
                 stx2.w(), sto.w(), ldg.w(), stsz.w(), stxs.w(), ldw.w(),
                 lde0.w(), lde1.w(), ldpc.w(), stpc.w()])

    with nc.Block() as block:
        @block.sync
        def _(eng):
            S.play("sync", eng)

        @block.tensor
        def _(eng):
            S.play("tensor", eng)

        @block.vector
        def _(eng):
            S.play("vector", eng)

        @block.scalar
        def _(eng):
            S.play("scalar", eng)

        @block.gpsimd
        def _(eng):
            S.play("gpsimd", eng)

    ctx.close()
    return nc


def kernel(**inputs):
    x = np.asarray(inputs["L0_attn_x"])[0]
    Wi = np.asarray(inputs["L0_attn_in_proj_w"])
    convw = np.asarray(inputs["L0_attn_conv_w"])[:, 0, :]
    convb = np.asarray(inputs["L0_attn_conv_b"])
    Wdt = np.asarray(inputs["L0_attn_dt_w"])
    dtb = np.asarray(inputs["L0_attn_dt_b"])
    WB = np.asarray(inputs["L0_attn_B_proj_w"])
    WC = np.asarray(inputs["L0_attn_C_proj_w"])
    Alog = np.asarray(inputs["L0_attn_A_log"])
    Dp = np.asarray(inputs["L0_attn_D_param"])
    Wo = np.asarray(inputs["L0_attn_out_proj_w"])
    Wg = np.asarray(inputs["L0_ffn_w_gate"])
    W1e = np.asarray(inputs["L0_ffn_w1_e"])
    W3e = np.asarray(inputs["L0_ffn_w3_e"])
    W2e = np.asarray(inputs["L0_ffn_w2_e"])
    W1s = np.asarray(inputs["L0_ffn_w1_s"])
    W3s = np.asarray(inputs["L0_ffn_w3_s"])
    W2s = np.asarray(inputs["L0_ffn_w2_s"])

    c = np.ascontiguousarray
    common = {
        "w_ip": c(Wi.T),
        "w_dt": c(Wdt.T),
        "w_bc": c(np.concatenate([WB.T, WC.T], axis=1)),
        "w_out": c(Wo.T),
        "w_gate": c(Wg.T),
        "w1e": c(W1e.transpose(0, 2, 1)),
        "w3e": c(W3e.transpose(0, 2, 1)),
        "w2e": c(W2e.transpose(0, 2, 1)),
        "w1s": c(W1s.T),
        "w3s": c(W3s.T),
        "w2s": c(W2s.T),
        "convw_r": c(convw.reshape(16, 128, KCONV).transpose(1, 0, 2)
                     .reshape(128, 16 * KCONV)),
        "convb_r": c(convb.reshape(16, 128).T),
        "dtb_r": c(dtb.reshape(16, 128).T),
        "alog_r": c(Alog.reshape(16, 128, DS).transpose(1, 0, 2)
                    .reshape(128, 16 * DS)),
        "dp_r": c(Dp.reshape(16, 128).T),
        "ident_in": np.eye(128, dtype=np.float32),
        "ones_row": np.ones((1, 128), dtype=np.float32),
        "ones_col_bf": np.ones((128, 1), dtype=BF),
        "fconst_in": np.tile(np.array([[1e-5, 1.0, -0.5, -1.0]],
                                      dtype=np.float32), (128, 1)),
    }
    sel16 = np.zeros((16, 16 * 128), dtype=np.float32)
    for n in range(16):
        sel16[n, n * 128:(n + 1) * 128] = 1.0
    common["sel16_in"] = sel16

    in_maps = []
    for ci in range(NCORES):
        lo = ci * TCORE
        xt = np.zeros((HALO + TCORE, DM), dtype=np.float32)
        s = max(0, lo - HALO)
        xt[HALO - (lo - s):] = x[s:lo + TCORE]
        sel8 = np.zeros((128, 8), dtype=np.float32)
        sel8[:, ci] = 1.0
        m = dict(common)
        m["x_tok"] = c(xt)
        m["sel8_in"] = sel8
        in_maps.append(m)

    nc = build_program()
    res = run_bass_kernel_spmd(nc, in_maps, core_ids=list(range(NCORES)))
    out = np.concatenate([np.asarray(r["out"]) for r in res.results], axis=0)
    return out.reshape(1, NCORES * TCORE, DM)


# version 19
# speedup vs baseline: 1.1322x; 1.1322x over previous
import numpy as np
import ml_dtypes
from contextlib import ExitStack

import concourse.bass as bass
import concourse.mybir as mybir
from concourse.bass_utils import run_bass_kernel_spmd

F32 = mybir.dt.float32
BF16 = mybir.dt.bfloat16
AF = mybir.ActivationFunctionType
ALU = mybir.AluOpType
AX = mybir.AxisListType
BF = ml_dtypes.bfloat16

NCORES = 8
TCORE = 2048
HALO = 128
DM = 1024
DI = 2048
DS = 16
KCONV = 4
NE = 16
FF = 512
FS = 1024

TT_A = 512
NT_A = TCORE // TT_A
TT_C1 = 256
NT_C1 = TCORE // TT_C1
TT_C2 = 512
NT_C2 = TCORE // TT_C2
TT_C3 = 256
NT_C3 = TCORE // TT_C3

ARENA_BYTES = 194560
ENGINES = ("sync", "tensor", "vector", "scalar", "gpsimd")


class Sem:
    __slots__ = ("h", "n", "name")

    def __init__(self, h, name):
        self.h = h
        self.n = 0
        self.name = name

    def w(self):
        return (self, self.n)


class Sched:
    def __init__(self):
        self.items = {e: [] for e in ENGINES}
        self._fence = {e: () for e in ENGINES}

    def fence(self, waits):
        ws = tuple(w for w in waits if w and w[1] > 0)
        for e in ENGINES:
            self._fence[e] = self._fence[e] + ws

    def add(self, eng, fn, waits=(), inc=None, amt=1):
        ws = [w for w in waits if w is not None and w[1] > 0]
        if self._fence[eng]:
            ws.extend(self._fence[eng])
            self._fence[eng] = ()
        if inc is not None:
            inc.n += amt
        self.items[eng].append((fn, tuple(ws), inc, amt))

    def play(self, eng_name, eng):
        waited = {}
        for fn, ws, inc, amt in self.items[eng_name]:
            for (s, v) in ws:
                k = id(s)
                if waited.get(k, 0) < v:
                    eng.wait_ge(s.h, v)
                    waited[k] = v
            ins = fn(eng)
            if inc is not None:
                ins.then_inc(inc.h, amt)


class Arena:
    def __init__(self, ap, nbytes):
        self.ap = ap
        self.nbytes = nbytes
        self.off = 0
        self.marks = []

    def push(self):
        self.marks.append(self.off)

    def pop(self):
        self.off = self.marks.pop()

    def alloc(self, free_elems, dtype, align=64):
        isz = 2 if dtype == BF16 else 4
        self.off = (self.off + align - 1) // align * align
        nb = free_elems * isz
        assert self.off + nb <= self.nbytes, (
            f"arena overflow: off={self.off} need={nb} cap={self.nbytes}")
        view = self.ap[:, self.off:self.off + nb].bitcast(dtype)
        self.off += nb
        return view


class Bank:
    __slots__ = ("ps", "wr", "rd")

    def __init__(self, ps):
        self.ps = ps
        self.wr = None
        self.rd = []


def build_program(debug=0):
    nc = bass.Bass("TRN2", target_bir_lowering=False)
    d_in = {}

    def din(name, shape, dtype=F32):
        d_in[name] = nc.dram_tensor(name, list(shape), dtype,
                                    kind="ExternalInput").ap()
        return d_in[name]

    x_tok = din("x_tok", [HALO + TCORE, DM])
    w_ip = din("w_ip", [DM, 2 * DI])
    w_dt = din("w_dt", [DI, DI])
    w_bc = din("w_bc", [DI, 2 * DS])
    w_out = din("w_out", [DI, DM])
    w_gate = din("w_gate", [DM, NE])
    w1e = din("w1e", [NE, DM, FF])
    w3e = din("w3e", [NE, DM, FF])
    w2e = din("w2e", [NE, FF, DM])
    w1s = din("w1s", [DM, FS])
    w3s = din("w3s", [DM, FS])
    w2s = din("w2s", [FS, DM])
    convw_r = din("convw_r", [128, 16 * KCONV])
    convb_r = din("convb_r", [128, 16])
    dtb_r = din("dtb_r", [128, 16])
    alog_r = din("alog_r", [128, 16 * DS])
    dp_r = din("dp_r", [128, 16])
    ident_in = din("ident_in", [128, 128])
    ones_row = din("ones_row", [1, 128])
    ones_col_bf = din("ones_col_bf", [128, 1], BF16)
    sel16_in = din("sel16_in", [16, 16 * 128])
    sel8_in = din("sel8_in", [128, 8])
    fconst_in = din("fconst_in", [128, 4])

    out_t = nc.dram_tensor("out", [TCORE, DM], F32, kind="ExternalOutput").ap()

    sp_xs = nc.dram_tensor("sp_xs", [16, 128, TCORE], BF16).ap()
    sp_sz = nc.dram_tensor("sp_sz", [16, 128, TCORE], BF16).ap()
    sp_xfm = nc.dram_tensor("sp_xfm", [8, 128, TCORE], F32).ap()
    sp_dt = nc.dram_tensor("sp_dt", [16, 128, TCORE], F32).ap()
    sp_dx = nc.dram_tensor("sp_dx", [16, 128, TCORE], F32).ap()
    sp_y = nc.dram_tensor("sp_y", [16, 128, TCORE], F32).ap()
    sp_x2 = nc.dram_tensor("sp_x2", [8, 128, TCORE], F32).ap()
    wexp_bf_dram = nc.dram_tensor("wexp_bf", [NE, 3, 128, 8 * FF], BF16).ap()
    wsh_bf_dram = nc.dram_tensor("wsh_bf", [3, 128, 8 * FS], BF16).ap()
    cc_in = nc.dram_tensor("cc_in", [128, 16 * DS], F32).ap()
    cc_out = nc.dram_tensor("cc_out", [NCORES * 128, 16 * DS], F32,
                            addr_space="Shared").ap()

    S = Sched()
    ctx = ExitStack()
    sems = {}
    for name in ("ld", "st", "pe", "ve", "se", "po", "cc", "st1", "sty",
                 "stx2", "sto", "ldg", "stsz", "stxs", "ldw", "lde0", "lde1",
                 "ldpc", "stpc"):
        sems[name] = Sem(ctx.enter_context(nc.semaphore(name)), name)
    (ld, st, pe, ve, se, po, ccs, st1, sty, stx2, sto, ldg, stsz, stxs,
     ldw, lde0, lde1, ldpc, stpc) = (
        sems[k] for k in ("ld", "st", "pe", "ve", "se", "po", "cc", "st1",
                          "sty", "stx2", "sto", "ldg", "stsz", "stxs", "ldw",
                          "lde0", "lde1", "ldpc", "stpc"))

    arena_h = ctx.enter_context(nc.sbuf_tensor("arena", [128, ARENA_BYTES],
                                               mybir.dt.uint8))
    A = Arena(arena_h[:, :], ARENA_BYTES)

    banks = []
    for i in range(8):
        ps = ctx.enter_context(nc.psum_tensor(f"ps{i}", [128, 512], F32))
        banks.append(Bank(ps[:, :]))

    def dma_in(dst, src, waits=()):
        S.add("sync", lambda e, d=dst, s=src: e.dma_start(out=d, in_=s),
              waits=waits, inc=ld, amt=16)
        return ld.w()

    def dma_out(dst, src, waits=(), sem=None):
        sem = sem if sem is not None else st
        S.add("sync", lambda e, d=dst, s=src: e.dma_start(out=d, in_=s),
              waits=waits, inc=sem, amt=16)
        return sem.w()

    def pop(fn, waits=()):
        S.add("gpsimd", fn, waits=waits, inc=po)
        return po.w()

    def vop(fn, waits=()):
        S.add("vector", fn, waits=waits, inc=ve)
        return ve.w()

    def sop(fn, waits=()):
        S.add("scalar", fn, waits=waits, inc=se)
        return se.w()

    def mms(bank, pairs, N, out_rows=128, waits=()):
        n = len(pairs)
        out_ap = bank.ps[0:out_rows, 0:N]
        for i in range(n):
            lt, rh = pairs[i]
            w = (list(waits) + list(bank.rd)) if i == 0 else ()
            S.add("tensor",
                  (lambda e, l=lt, r=rh, o=out_ap, s=(i == 0), p=(i == n - 1):
                   e.matmul(out=o, lhsT=l, rhs=r, start=s, stop=p)),
                  waits=w, inc=pe if i == n - 1 else None)
        bank.wr = pe.w()
        bank.rd = []
        return bank.wr

    def transpose_into(bank, in_ap, ident_ap, waits=()):
        pr, fr = in_ap.shape[0], in_ap.shape[1]
        w = list(waits) + list(bank.rd)
        S.add("tensor",
              (lambda e, i=in_ap, d=ident_ap, o=bank.ps[0:fr, 0:pr]:
               e.transpose(out=o, in_=i, identity=d)),
              waits=w, inc=pe)
        bank.wr = pe.w()
        bank.rd = []
        return bank.wr

    def read_bank(bank, eng, fn, waits=()):
        s = {"vector": ve, "scalar": se}[eng]
        S.add(eng, fn, waits=list(waits) + ([bank.wr] if bank.wr else []), inc=s)
        bank.rd.append(s.w())
        return s.w()

    buf_rd = {}

    def load_cast(hbm_ap, rows, cols, dst_bf, stage, chunk_cols=None, waits=()):
        cw = chunk_cols or cols
        last = buf_rd.get(id(stage))
        extra = list(waits)
        for kc in range(rows // 128):
            for c0 in range(0, cols, cw):
                S.add("sync", lambda e, d=stage[:, 0:cw],
                      s2=hbm_ap[kc * 128:(kc + 1) * 128, c0:c0 + cw]:
                      e.dma_start(out=d, in_=s2),
                      waits=([last] if last else []) + extra, inc=ldw, amt=16)
                w = ldw.w()
                extra = []
                last = vop(lambda e, o=dst_bf[:, kc * cols + c0: kc * cols + c0 + cw],
                           i=stage[:, 0:cw]: e.tensor_copy(out=o, in_=i), waits=[w])
        buf_rd[id(stage)] = last
        return last

    ident = A.alloc(128, F32)
    sel16 = A.alloc(16 * 128, F32)
    sel8 = A.alloc(8, F32)
    onesr = A.alloc(128, F32)
    onescb = A.alloc(1, BF16)
    A_sb = A.alloc(16 * DS, F32)
    convw = A.alloc(16 * KCONV, F32)
    convb = A.alloc(16, F32)
    dtbc = A.alloc(16, F32)
    dpc = A.alloc(16, F32)
    state = A.alloc(16 * DS, F32)

    fconst = A.alloc(4, F32)
    dma_in(fconst, fconst_in)
    for _i, _v in enumerate((1e-5, 1.0, -0.5, -1.0)):
        nc.const_aps.register(F32, _v, fconst[:, _i:_i + 1])
    dma_in(ident, ident_in)
    dma_in(sel16[0:16, :], sel16_in)
    dma_in(sel8, sel8_in)
    dma_in(onesr[0:1, :], ones_row)
    dma_in(onescb, ones_col_bf)
    dma_in(convw, convw_r)
    dma_in(convb, convb_r)
    dma_in(dtbc, dtb_r)
    dma_in(dpc, dp_r)
    wal = dma_in(A_sb, alog_r)
    wse = sop(lambda e: e.activation(out=A_sb, in_=A_sb, func=AF.Exp), waits=[wal])
    wA = vop(lambda e: e.tensor_scalar_mul(out=A_sb, in0=A_sb, scalar1=-1.0),
             waits=[wse])
    consts_ready = [ld.w(), wA]
    ln_eps = 1e-5

    pass

    A.push()
    w_ip_bf = A.alloc(8 * 2 * DI, BF16)
    stage8 = A.alloc(2048, F32)
    xbf = stage8.bitcast(BF16)[:, 0:8 * TT_A]
    x_tok_sb = [A.alloc(DM, F32), A.alloc(DM, F32)]
    ldx = None
    x_fm = A.alloc(8 * TT_A, F32)
    xp_sb = A.alloc(16 * (TT_A + 3), F32)
    xs_f = A.alloc(16 * TT_A, F32)
    szxq = A.alloc(16 * TT_A, BF16)
    lnrow = szxq.bitcast(F32)[:, 4 * TT_A: 8 * TT_A]

    S.fence(consts_ready)
    buf_rd["ld0"] = ld.w()
    wipr = load_cast(w_ip, DM, 2 * DI, w_ip_bf, stage8, chunk_cols=2048)

    def emit_ln(x_ap, tw, xout_bf, lnrow_ap, ttsz, rw_waits, sq_waits, ip_waits):
        xsq = szxq[:, 0:8 * tw]
        for db in range(8):
            vop(lambda e, o=xout_bf[:, db * tw:(db + 1) * tw],
                i=x_ap[:, db * tw:(db + 1) * tw]: e.tensor_copy(out=o, in_=i),
                waits=(list(rw_waits) + [buf_rd.get("a1xbf_pe")])
                if db == 0 else ())
        xbf_ready = ve.w()
        for db in range(8):
            sop(lambda e, o=xsq[:, db * tw:(db + 1) * tw],
                i=x_ap[:, db * tw:(db + 1) * tw]: e.activation(
                    out=o, in_=i, func=AF.Square),
                waits=(list(rw_waits) + list(sq_waits)) if db == 0 else ())
        xsq_ready = se.w()
        s1b, s2b = banks[2], banks[3]
        mms(s1b, [(onescb, xout_bf[:, db * tw:(db + 1) * tw]) for db in range(8)],
            tw, out_rows=1, waits=[xbf_ready])
        mms(s2b, [(onescb, xsq[:, db * tw:(db + 1) * tw]) for db in range(8)],
            tw, out_rows=1, waits=[xsq_ready])
        m_row = lnrow_ap[0:1, 0:tw]
        var_row = lnrow_ap[0:1, ttsz:ttsz + tw]
        rstd_row = lnrow_ap[0:1, 2 * ttsz:2 * ttsz + tw]
        mrs_row = lnrow_ap[0:1, 3 * ttsz:3 * ttsz + tw]
        read_bank(s1b, "vector", lambda e, o=m_row, i=s1b.ps[0:1, 0:tw]:
                  e.tensor_scalar_mul(out=o, in0=i, scalar1=1.0 / DM),
                  waits=list(ip_waits) + [buf_rd.get("lnrow_pe")])
        vop(lambda e, o=var_row, i=m_row: e.tensor_mul(out=o, in0=i, in1=i),
            waits=[ve.w()])
        read_bank(s2b, "vector", lambda e, o=var_row, i=s2b.ps[0:1, 0:tw],
                  i1=var_row: e.scalar_tensor_tensor(
                      out=o, in0=i, scalar=1.0 / DM, in1=i1,
                      op0=ALU.mult, op1=ALU.subtract), waits=[ve.w()])
        wv = ve.w()
        w1 = sop(lambda e, o=rstd_row, i=var_row: e.activation(
            out=o, in_=i, func=AF.Ln, bias=ln_eps), waits=[wv, xsq_ready])
        w2 = sop(lambda e, o=rstd_row: e.activation(
            out=o, in_=o, func=AF.Exp, scale=-0.5), waits=[w1])
        wm = vop(lambda e, o=mrs_row, i=m_row, i1=rstd_row: e.tensor_mul(
            out=o, in0=i, in1=i1), waits=[w2, ve.w()])
        rb, mb = banks[4], banks[5]
        mms(rb, [(onesr[0:1, :], rstd_row)], tw, waits=[wm])
        mms(mb, [(onesr[0:1, :], mrs_row)], tw, waits=[wm])
        buf_rd["lnrow_pe"] = pe.w()
        for db in range(8):
            read_bank(rb, "vector", lambda e,
                      o=x_ap[:, db * tw:(db + 1) * tw],
                      i=x_ap[:, db * tw:(db + 1) * tw],
                      p=rb.ps[:, 0:tw]: e.tensor_mul(out=o, in0=i, in1=p))
        for db in range(8):
            read_bank(mb, "vector", lambda e,
                      o=xout_bf[:, db * tw:(db + 1) * tw],
                      i=x_ap[:, db * tw:(db + 1) * tw],
                      p=mb.ps[:, 0:tw]: e.tensor_sub(out=o, in0=i, in1=p))
        return ve.w(), pe.w()

    def a1_tile(t0, tw, full):
        ntb = tw // 128
        for tb in range(ntb):
            par = tb % 2
            sbuf = x_tok_sb[par]
            xsem = (st, po)[par]
            S.add("sync", lambda e, d=sbuf,
                  s2=x_tok[t0 + tb * 128: t0 + (tb + 1) * 128, :]:
                  e.dma_start(out=d, in_=s2),
                  waits=[buf_rd.get(("xtok", par)), buf_rd.get("ld0")],
                  inc=xsem, amt=16)
            w = xsem.w()
            for db in range(8):
                bank = banks[db % 2]
                transpose_into(bank, sbuf[:, db * 128:(db + 1) * 128], ident,
                               waits=[w])
                if db == 7:
                    buf_rd[("xtok", par)] = pe.w()
                read_bank(bank, "scalar", lambda e,
                          o=x_fm[:, db * tw + tb * 128: db * tw + (tb + 1) * 128],
                          p=bank.ps[0:128, 0:128]: e.copy(out=o, in_=p),
                          waits=([buf_rd.get("xfmbuf_ve"), buf_rd.get("xfmbuf_st")]
                                 if (tb == 0 and db == 0) else ()))
        xfm_ready = se.w()
        xfm_spilled = None
        if full:
            for a in range(8):
                xfm_spilled = dma_out(sp_xfm[a, :, t0 - HALO: t0 - HALO + tw],
                                      x_fm[:, a * tw:(a + 1) * tw],
                                      waits=[xfm_ready] if a == 0 else (),
                                      sem=st1)
        rw = [xfm_ready] + ([xfm_spilled] if xfm_spilled else [])
        xln_ready, ln_pe = emit_ln(
            x_fm, tw, xbf, lnrow, TT_A, rw,
            [buf_rd.get("szxq")], [buf_rd.get("lnrow_st")])
        buf_rd["lnrow_st"] = None
        nob = 32 if full else 16
        for ob in range(nob):
            bank = banks[ob % 2]
            mms(bank, [(w_ip_bf[:, kc * 2 * DI + ob * 128:
                                kc * 2 * DI + (ob + 1) * 128],
                        xbf[:, kc * tw:(kc + 1) * tw]) for kc in range(8)],
                tw, waits=[xln_ready, wipr])
            if ob < 16:
                read_bank(bank, "scalar", lambda e,
                          o=xp_sb[:, ob * (TT_A + 3) + 3:
                                  ob * (TT_A + 3) + 3 + tw],
                          p=bank.ps[:, 0:tw]: e.copy(out=o, in_=p),
                          waits=[buf_rd.get("xp")] if ob == 0 else ())
            else:
                read_bank(bank, "scalar", lambda e,
                          o=szxq[:, (ob - 16) * tw:(ob - 15) * tw],
                          p=bank.ps[:, 0:tw]: e.activation(
                              out=o, in_=p, func=AF.Silu),
                          waits=[buf_rd.get("szxq")] if ob == 16 else ())
        buf_rd["a1xbf_pe"] = pe.w()
        xp_ready = se.w()
        sz_ready = se.w()
        if full:
            sz_done = None
            for a in range(16):
                sz_done = dma_out(sp_sz[a, :, t0 - HALO: t0 - HALO + tw],
                                  szxq[:, a * tw:(a + 1) * tw],
                                  waits=[sz_ready] if a == 0 else (), sem=stsz)
            buf_rd["szxq"] = sz_done
            buf_rd["lnrow_st"] = sz_done
            wxs0 = buf_rd.get("xs")
            for g in range(16):
                base = g * (TT_A + 3)
                vop(lambda e, o=xs_f[:, g * tw:(g + 1) * tw],
                    i=xp_sb[:, base:base + tw],
                    s=convw[:, g * KCONV:g * KCONV + 1]: e.tensor_scalar(
                        out=o, in0=i, scalar1=s, scalar2=None, op0=ALU.mult),
                    waits=[xp_ready, wxs0] if g == 0 else ())
                for k in range(1, KCONV):
                    vop(lambda e, o=xs_f[:, g * tw:(g + 1) * tw],
                        i=xp_sb[:, base + k:base + k + tw],
                        s=convw[:, g * KCONV + k:g * KCONV + k + 1]:
                        e.scalar_tensor_tensor(out=o, in0=i, scalar=s, in1=o,
                                               op0=ALU.mult, op1=ALU.add))
            conv_done = ve.w()
            for g in range(16):
                sop(lambda e, o=xs_f[:, g * tw:(g + 1) * tw],
                    b=convb[:, g:g + 1]: e.activation(
                        out=o, in_=o, func=AF.Silu, bias=b),
                    waits=[conv_done] if g == 0 else ())
            xs_ready = se.w()
            xs_bf_sp = x_fm.bitcast(BF16)[:, 0:16 * tw]
            for g in range(16):
                vop(lambda e, o=xs_bf_sp[:, g * tw:(g + 1) * tw],
                    i=xs_f[:, g * tw:(g + 1) * tw]: e.tensor_copy(out=o, in_=i),
                    waits=[xs_ready] if g == 0 else ())
            wcast = ve.w()
            buf_rd["xs"] = wcast
            buf_rd["xfmbuf_ve"] = wcast
            wxsp = None
            for a in range(16):
                wxsp = dma_out(sp_xs[a, :, t0 - HALO: t0 - HALO + tw],
                               xs_bf_sp[:, a * tw:(a + 1) * tw],
                               waits=[wcast] if a == 0 else (), sem=stxs)
            buf_rd["xfmbuf_st"] = wxsp
        for g in range(16):
            b = g * (TT_A + 3)
            vop(lambda e, o=xp_sb[:, b:b + 3], i=xp_sb[:, b + tw:b + tw + 3]:
                e.tensor_copy(out=o, in_=i),
                waits=[xp_ready] if g == 0 else ())
        buf_rd["xp"] = ve.w()

    a1_tile(0, HALO, full=False)
    for ti in range(NT_A):
        a1_tile(HALO + ti * TT_A, TT_A, full=True)
    S.fence([ve.w(), se.w(), pe.w(), st.w()])
    A.pop()

    A.push()
    w_dt_bf = A.alloc(16 * DI, BF16)
    w_bc_bf = A.alloc(16 * 2 * DS, BF16)
    identb = A.alloc(128, BF16)
    sel16b = A.alloc(16 * 128, BF16)
    vop(lambda e: e.tensor_copy(out=identb, in_=ident))
    vop(lambda e: e.tensor_copy(out=sel16b[0:16, :], in_=sel16[0:16, :]))
    y_acc = A.alloc(16 * TT_A, F32)
    xs_bf = y_acc.bitcast(BF16)[:, 0:16 * TT_A]
    dt_sb = A.alloc(16 * TT_A, F32)
    dx_sb = A.alloc(16 * TT_A, BF16)
    e1_one = A.alloc(TT_A, F32)
    e1_sb = [e1_one, e1_one]
    a_t = [A.alloc(TT_A, F32) for _ in range(4)]
    _bt = A.alloc(TT_A, BF16)
    b_t = [_bt, _bt]
    h_t = A.alloc(TT_A, BF16)
    tmp_t = [A.alloc(TT_A, BF16) for _ in range(4)]
    bp_sb = A.alloc(TT_A, BF16)
    cp_sb = A.alloc(TT_A, BF16)
    bb_sb = [A.alloc(TT_A, BF16), A.alloc(TT_A, BF16)]
    cb_sbuf = [A.alloc(TT_A, BF16), A.alloc(TT_A, BF16)]
    g_sb = dx_sb.bitcast(F32)[:, 0:8 * 16 * DS]

    wstage = dt_sb
    load_cast(w_dt, DI, DI, w_dt_bf, wstage[:, 0:DI], chunk_cols=1024)
    wdtr = load_cast(w_bc, DI, 2 * DS, w_bc_bf, wstage[:, 0:2 * DS])

    a_rd = [None] * 4
    state_w = {}

    def a2_tile(xs_src, tw, do_y, first_scan_wait=None):
        wx = None
        for a in range(16):
            wx = dma_in(xs_bf[:, a * tw:(a + 1) * tw], xs_src(a),
                        waits=[buf_rd.get("yspill"), buf_rd.get("a2dtx")]
                        if a == 0 else ())
        for ob in range(16):
            bank = banks[ob % 2]
            mms(bank, [(w_dt_bf[:, kc * DI + ob * 128: kc * DI + (ob + 1) * 128],
                        xs_bf[:, kc * tw:(kc + 1) * tw]) for kc in range(16)],
                tw, waits=[wx, wdtr])
            e1 = e1_sb[ob % 2]
            we = read_bank(bank, "scalar", lambda e, o=e1[:, 0:tw],
                           p=bank.ps[:, 0:tw], b=dtbc[:, ob:ob + 1]:
                           e.activation(out=o, in_=p, func=AF.Exp, bias=b))
            sop(lambda e, o=dt_sb[:, ob * tw:(ob + 1) * tw], i=e1[:, 0:tw]:
                e.activation(out=o, in_=i, func=AF.Ln, bias=1.0), waits=[we])
        dt_ready = se.w()
        bb_, cb_ = banks[2], banks[3]
        mms(bb_, [(w_bc_bf[:, kc * 2 * DS: kc * 2 * DS + DS],
                   xs_bf[:, kc * tw:(kc + 1) * tw]) for kc in range(16)],
            tw, out_rows=16, waits=[wx, wdtr])
        mms(cb_, [(w_bc_bf[:, kc * 2 * DS + DS: (kc + 1) * 2 * DS],
                   xs_bf[:, kc * tw:(kc + 1) * tw]) for kc in range(16)],
            tw, out_rows=16, waits=[wx])
        read_bank(bb_, "vector", lambda e, o=bp_sb[0:16, 0:tw],
                  p=bb_.ps[0:16, 0:tw]: e.tensor_copy(out=o, in_=p),
                  waits=[buf_rd.get("a2bp"), buf_rd.get("a2bp_pe")])
        read_bank(cb_, "vector", lambda e, o=cp_sb[0:16, 0:tw],
                  p=cb_.ps[0:16, 0:tw]: e.tensor_copy(out=o, in_=p))
        bp_ready = ve.w()
        for g in range(16):
            vop(lambda e, o=dx_sb[:, g * tw:(g + 1) * tw],
                i=dt_sb[:, g * tw:(g + 1) * tw],
                i1=xs_bf[:, g * tw:(g + 1) * tw]: e.tensor_mul(
                    out=o, in0=i, in1=i1),
                waits=[dt_ready] if g == 0 else ())
        dx_ready = ve.w()
        passes = ([range(0, 4), range(4, 8), range(8, 12), range(12, 16)]
                  if do_y else [range(16)])
        spill_waits = []

        def emit_sel(n, par):
            bbk, cbk = banks[4 + par], banks[6 + par]
            mms(bbk, [(sel16b[0:16, n * 128:(n + 1) * 128],
                       bp_sb[0:16, 0:tw])], tw, waits=[bp_ready])
            if do_y:
                mms(cbk, [(sel16b[0:16, n * 128:(n + 1) * 128],
                           cp_sb[0:16, 0:tw])], tw, waits=[bp_ready])

        def emit_evac(n, par):
            bbk, cbk = banks[4 + par], banks[6 + par]
            wbe = read_bank(bbk, "vector", lambda e, o=bb_sb[par][:, 0:tw],
                            p=bbk.ps[:, 0:tw]: e.tensor_copy(out=o, in_=p))
            wce = None
            if do_y:
                wce = read_bank(cbk, "vector", lambda e, o=cb_sbuf[par][:, 0:tw],
                                p=cbk.ps[:, 0:tw]: e.tensor_copy(out=o, in_=p))
            return wbe, wce

        emit_sel(0, 0)
        for pi, gl in enumerate(passes):
            for n in range(DS):
                par = n % 2
                bbs = bb_sb[par]
                cbs = cb_sbuf[par]
                if n + 1 < DS:
                    emit_sel(n + 1, (n + 1) % 2)
                elif pi + 1 < len(passes):
                    emit_sel(0, 0)
                wbe, wce = emit_evac(n, par)
                for gi, g in enumerate(gl):
                    ab = a_t[g % 4]
                    wa = sop(lambda e, o=ab[:, 0:tw],
                             i=dt_sb[:, g * tw:(g + 1) * tw],
                             s=A_sb[:, g * DS + n: g * DS + n + 1]: e.activation(
                                 out=o, in_=i, func=AF.Exp, scale=s),
                             waits=[w for w in (a_rd[g % 4],) if w])
                    bt = b_t[g % 2]
                    vop(lambda e, o=bt[:, 0:tw],
                        i=dx_sb[:, g * tw:(g + 1) * tw],
                        i1=bbs[:, 0:tw]: e.tensor_tensor(
                            out=o, in0=i, in1=i1, op=ALU.mult),
                        waits=[wbe] if g == gl[0] else ())
                    col = state[:, g * DS + n: g * DS + n + 1]
                    extra = []
                    if first_scan_wait is not None and n == 0 and gi == 0:
                        extra.append(first_scan_wait)
                    vop(lambda e, d0=ab[:, 0:tw], d1=bt[:, 0:tw], o=h_t[:, 0:tw],
                        c=col: e.tensor_tensor_scan(
                            out=o, data0=d0, data1=d1, initial=c,
                            op0=ALU.mult, op1=ALU.add),
                        waits=[wa, state_w.get((n, g))] + extra)
                    a_rd[g % 4] = ve.w()
                    if do_y:
                        ybank = banks[gi]
                        tp_ = tmp_t[g % 4]
                        wcm = vop(lambda e, o=tp_[:, 0:tw], i=h_t[:, 0:tw],
                                  i1=cbs[:, 0:tw]: e.tensor_mul(
                                      out=o, in0=i, in1=i1),
                                  waits=[wce, buf_rd.get(("tmprd", g % 4))])
                        wmm = list(ybank.rd) if n == 0 else []
                        S.add("tensor",
                              (lambda e, o=ybank.ps[:, 0:tw], l=identb,
                               r=tp_[:, 0:tw], st_=(n == 0), sp=(n == DS - 1):
                               e.matmul(out=o, lhsT=l, rhs=r, start=st_,
                                        stop=sp, skip_group_check=True)),
                              waits=[wcm] + wmm, inc=pe)
                        buf_rd[("tmprd", g % 4)] = pe.w()
                        if n == 0:
                            ybank.rd = []
                        if n == DS - 1:
                            ybank.wr = pe.w()
                    state_w[(n, g)] = pop(
                        lambda e, c=col, i=h_t[:, tw - 1:tw]: e.tensor_copy(
                            out=c, in_=i), waits=[ve.w()])

            if do_y:
                for gi, g in enumerate(gl):
                    ybank = banks[gi]
                    read_bank(ybank, "vector", lambda e,
                              o=y_acc[:, g * tw:(g + 1) * tw],
                              p=ybank.ps[:, 0:tw]: e.tensor_copy(out=o, in_=p),
                              waits=[buf_rd.get("ysp_all")]
                              if (gi == 0 and gl[0] == 0) else ())
                spill_waits.append(ve.w())
        buf_rd["a2bp"] = ve.w()
        buf_rd["a2bp_pe"] = pe.w()
        return spill_waits

    vop(lambda e: e.memset(state, 0.0))
    ws0 = ve.w()
    a2_tile(lambda a: sp_xs[a, :, TCORE - HALO:TCORE], HALO, do_y=False,
            first_scan_wait=ws0)
    tail_done = ve.w()
    w1 = dma_out(cc_in, state, waits=[tail_done], sem=sty)
    S.add("gpsimd", lambda e: e.collective_compute(
        "AllGather", ALU.bypass, replica_groups=[list(range(NCORES))],
        ins=[cc_in], outs=[cc_out]), waits=[w1], inc=ccs, amt=1)
    S.add("sync", lambda e, d=g_sb.rearrange("p (r c) -> p r c", r=NCORES),
          s2=cc_out.rearrange("(r p) c -> p r c", p=128):
          e.dma_start(out=d, in_=s2), waits=[ccs.w()], inc=ldg, amt=16)
    wg = ldg.w()
    vop(lambda e: e.memset(state, 0.0), waits=[tail_done, w1])
    for j in range(1, NCORES):
        vop(lambda e, i=g_sb[:, (j - 1) * 256: j * 256], s=sel8[:, j:j + 1]:
            e.scalar_tensor_tensor(out=state, in0=i, scalar=s, in1=state,
                                   op0=ALU.mult, op1=ALU.add),
            waits=[wg, ve.w()] if j == 1 else [ve.w()])
    hinit_ready = ve.w()

    for ti in range(NT_A):
        t0 = ti * TT_A
        sw = a2_tile(lambda a, t0=t0: sp_xs[a, :, t0:t0 + TT_A], TT_A, do_y=True,
                     first_scan_wait=hinit_ready if ti == 0 else None)
        glists = [range(0, 4), range(4, 8), range(8, 12), range(12, 16)]
        wy = None
        for gl, w in zip(glists, sw):
            for a in gl:
                wy = dma_out(sp_y[a, :, t0:t0 + TT_A],
                             y_acc[:, a * TT_A:(a + 1) * TT_A],
                             waits=[w] if a == gl[0] else (), sem=sty)
        buf_rd["yspill"] = wy
        buf_rd["ysp_all"] = wy
        buf_rd["a2dtx"] = ve.w()
    S.fence([ve.w(), se.w(), pe.w(), st.w(), ld.w(), po.w()])
    A.pop()

    xln2bf = A.alloc(8 * TCORE, BF16)
    comb16 = A.alloc(TCORE, F32)
    A.push()
    w_out_bf = A.alloc(16 * DM, BF16)
    wg_sb = A.alloc(8 * NE, F32)
    stage4b = A.alloc(1024, F32)
    y_f = A.alloc(16 * TT_C1, F32)
    xs_c = A.alloc(16 * TT_C1, BF16)
    sz_c = A.alloc(16 * TT_C1, BF16)
    ybf = A.alloc(16 * TT_C1, BF16)
    xfm_c = A.alloc(8 * TT_C1, F32)
    x2_c = A.alloc(8 * TT_C1, F32)
    xl2f = A.alloc(8 * TT_C1, F32)
    lnrow2 = A.alloc(4 * TT_C1, F32)
    srow = A.alloc(TT_C1, F32)
    sT = A.alloc(2 * NE, F32)
    gte = A.alloc(512, F32)
    ntb1 = TT_C1 // 128
    B2 = ntb1

    def vopE(fn, waits=()):
        return vop(fn, waits=list(waits) + [ve.w()])

    load_cast(w_out, DI, DM, w_out_bf, stage4b, chunk_cols=1024)

    for kc in range(8):
        S.add("sync", lambda e, d=wg_sb[:, kc * NE:(kc + 1) * NE],
              s2=w_gate[kc * 128:(kc + 1) * 128, :]: e.dma_start(out=d, in_=s2),
              inc=ldg, amt=16)
    wgw = ldg.w()

    for ti in range(NT_C1):
        t0 = ti * TT_C1
        tw = TT_C1
        prev = buf_rd.get("c1prev") or []
        wy = None
        for a in range(16):
            dma_in(y_f[:, a * tw:(a + 1) * tw], sp_y[a, :, t0:t0 + tw],
                   waits=list(prev) if a == 0 else ())
            dma_in(xs_c[:, a * tw:(a + 1) * tw], sp_xs[a, :, t0:t0 + tw])
            wy = dma_in(sz_c[:, a * tw:(a + 1) * tw], sp_sz[a, :, t0:t0 + tw])
        wxf = None
        for a in range(8):
            wxf = dma_in(xfm_c[:, a * tw:(a + 1) * tw], sp_xfm[a, :, t0:t0 + tw])
        for g in range(16):
            vop(lambda e, o=y_f[:, g * tw:(g + 1) * tw],
                i=xs_c[:, g * tw:(g + 1) * tw], s=dpc[:, g:g + 1]:
                e.scalar_tensor_tensor(out=o, in0=i, scalar=s, in1=o,
                                       op0=ALU.mult, op1=ALU.add),
                waits=[wxf] if g == 0 else ())
        for g in range(16):
            vop(lambda e, o=ybf[:, g * tw:(g + 1) * tw],
                i=y_f[:, g * tw:(g + 1) * tw],
                i1=sz_c[:, g * tw:(g + 1) * tw]: e.tensor_mul(
                    out=o, in0=i, in1=i1))
        ybf_ready = ve.w()
        for db in range(8):
            bank = banks[db % 2]
            mms(bank, [(w_out_bf[:, kc * DM + db * 128: kc * DM + (db + 1) * 128],
                        ybf[:, kc * tw:(kc + 1) * tw]) for kc in range(16)],
                tw, waits=[ybf_ready])
            read_bank(bank, "vector", lambda e, o=x2_c[:, db * tw:(db + 1) * tw],
                      i=xfm_c[:, db * tw:(db + 1) * tw], p=bank.ps[:, 0:tw]:
                      e.tensor_add(out=o, in0=i, in1=p),
                      waits=[wxf, buf_rd.get("c1x2sp")] if db == 0 else ())
        x2_ready = ve.w()
        wsp = None
        for a in range(8):
            wsp = dma_out(sp_x2[a, :, t0:t0 + tw], x2_c[:, a * tw:(a + 1) * tw],
                          waits=[x2_ready] if a == 0 else (), sem=stx2)
        buf_rd["c1x2sp"] = wsp
        xsq2 = ybf
        xb2 = xs_c
        for db in range(8):
            sop(lambda e, o=xsq2[:, db * tw:(db + 1) * tw],
                i=x2_c[:, db * tw:(db + 1) * tw]: e.activation(
                    out=o, in_=i, func=AF.Square),
                waits=[x2_ready] if db == 0 else ())
        xsq_ready = se.w()
        for db in range(8):
            vop(lambda e, o=xb2[:, db * tw:(db + 1) * tw],
                i=x2_c[:, db * tw:(db + 1) * tw]: e.tensor_copy(out=o, in_=i),
                waits=[x2_ready] if db == 0 else ())
        xb_ready = ve.w()
        s1b, s2b = banks[2], banks[3]
        mms(s1b, [(onescb, xb2[:, db * tw:(db + 1) * tw]) for db in range(8)],
            tw, out_rows=1, waits=[xb_ready])
        mms(s2b, [(onescb, xsq2[:, db * tw:(db + 1) * tw]) for db in range(8)],
            tw, out_rows=1, waits=[xsq_ready])
        m_row = lnrow2[0:1, 0:tw]
        var_row = lnrow2[0:1, TT_C1:TT_C1 + tw]
        rstd_row = lnrow2[0:1, 2 * TT_C1:2 * TT_C1 + tw]
        mrs_row = lnrow2[0:1, 3 * TT_C1:3 * TT_C1 + tw]
        buf_rd["c1prev"] = [ve.w(), se.w(), pe.w()]
        read_bank(s1b, "vector", lambda e, o=m_row, p=s1b.ps[0:1, 0:tw]:
                  e.tensor_scalar_mul(out=o, in0=p, scalar1=1.0 / DM))
        vop(lambda e, o=var_row, i=m_row: e.tensor_mul(out=o, in0=i, in1=i),
            waits=[ve.w()])
        read_bank(s2b, "vector", lambda e, o=var_row, p=s2b.ps[0:1, 0:tw],
                  i1=var_row: e.scalar_tensor_tensor(
                      out=o, in0=p, scalar=1.0 / DM, in1=i1,
                      op0=ALU.mult, op1=ALU.subtract), waits=[ve.w()])
        wv = ve.w()
        wl1 = sop(lambda e, o=rstd_row, i=var_row: e.activation(
            out=o, in_=i, func=AF.Ln, bias=ln_eps), waits=[wv, xsq_ready])
        wl2 = sop(lambda e, o=rstd_row: e.activation(
            out=o, in_=o, func=AF.Exp, scale=-0.5), waits=[wl1])
        wm = vop(lambda e, o=mrs_row, i=m_row, i1=rstd_row: e.tensor_mul(
            out=o, in0=i, in1=i1), waits=[wl2, ve.w()])
        rb, mb = banks[4], banks[5]
        mms(rb, [(onesr[0:1, :], rstd_row)], tw, waits=[wm])
        mms(mb, [(onesr[0:1, :], mrs_row)], tw, waits=[wm])
        for db in range(8):
            read_bank(rb, "vector", lambda e, o=xl2f[:, db * tw:(db + 1) * tw],
                      i=x2_c[:, db * tw:(db + 1) * tw], p=rb.ps[:, 0:tw]:
                      e.tensor_mul(out=o, in0=i, in1=p))
        for db in range(8):
            read_bank(mb, "vector", lambda e, o=xl2f[:, db * tw:(db + 1) * tw],
                      p=mb.ps[:, 0:tw]: e.tensor_sub(out=o, in0=o, in1=p))
        xl2_ready = ve.w()
        for db in range(8):
            vop(lambda e, o=xln2bf[:, db * TCORE + t0: db * TCORE + t0 + tw],
                i=xl2f[:, db * tw:(db + 1) * tw]: e.tensor_copy(out=o, in_=i))
        gb = banks[6]
        mms(gb, [(wg_sb[:, kc * NE:(kc + 1) * NE], xl2f[:, kc * tw:(kc + 1) * tw])
                 for kc in range(8)], tw, out_rows=16, waits=[xl2_ready, wgw])
        wsg = read_bank(gb, "scalar", lambda e, o=srow[0:16, 0:tw],
                        p=gb.ps[0:16, 0:tw]: e.activation(
                            out=o, in_=p, func=AF.Exp, scale=-1.0))
        vop(lambda e, o=srow[0:16, 0:tw]: e.tensor_scalar_add(
            out=o, in0=o, scalar1=1.0), waits=[wsg])
        wsg = vop(lambda e, o=srow[0:16, 0:tw]: e.reciprocal(out=o, in_=o),
                  waits=[ve.w()])
        for tb in range(ntb1):
            bank = banks[7]
            transpose_into(bank, srow[0:16, tb * 128:(tb + 1) * 128],
                           ident[0:16, 0:16], waits=[wsg])
            read_bank(bank, "vector", lambda e, o=sT[:, tb * NE:(tb + 1) * NE],
                      p=bank.ps[0:128, 0:16]: e.tensor_copy(out=o, in_=p))
        wst = ve.w()
        sT3 = sT.rearrange("p (b e) -> p b e", b=B2)
        v4 = sT.rearrange("p (b g j) -> p b g j", b=B2, g=4)
        npr = B2 * 4 * 2
        pmx = gte[:, 0:npr].rearrange("p (b g i) -> p b g i", b=B2, g=4)
        pmn = gte[:, npr:2 * npr].rearrange("p (b g i) -> p b g i", b=B2, g=4)
        gs = gte[:, 2 * npr:2 * npr + B2 * 4].rearrange("p (b g) -> p b g", b=B2)
        base = 2 * npr + B2 * 4
        t_a = gte[:, base:base + B2]
        t_b = gte[:, base + B2:base + 2 * B2]
        g2 = gte[:, base + 2 * B2:base + 3 * B2]
        rmax = gte[:, base + 3 * B2:base + 4 * B2]
        base += 4 * B2
        alw4 = gte[:, base:base + 4 * B2].rearrange("p (b g) -> p b g", b=B2)
        base += 4 * B2
        mskF = gte[:, base:base + 16 * B2]
        msk = mskF.rearrange("p (b e) -> p b e", b=B2)
        msk4 = mskF.rearrange("p (b g j) -> p b g j", b=B2, g=4)
        base += 16 * B2
        cur = gte[:, base:base + 16 * B2].rearrange("p (b e) -> p b e", b=B2)
        base += 16 * B2
        eqm = gte[:, base:base + 16 * B2].rearrange("p (b e) -> p b e", b=B2)
        base += 16 * B2
        wselF = gte[:, base:base + 16 * B2]
        wsel = wselF.rearrange("p (b e) -> p b e", b=B2)
        base += 16 * B2
        wsum = gte[:, base:base + B2]
        BIG = 1.0e30
        vopE(lambda e: e.tensor_max(out=pmx[:, :, :, 0], in0=v4[:, :, :, 0],
                                   in1=v4[:, :, :, 1]), waits=[wst])
        vopE(lambda e: e.tensor_max(out=pmx[:, :, :, 1], in0=v4[:, :, :, 2],
                                   in1=v4[:, :, :, 3]))
        vopE(lambda e: e.tensor_tensor(out=pmn[:, :, :, 0], in0=v4[:, :, :, 0],
                                      in1=v4[:, :, :, 1], op=ALU.min))
        vopE(lambda e: e.tensor_tensor(out=pmn[:, :, :, 1], in0=v4[:, :, :, 2],
                                      in1=v4[:, :, :, 3], op=ALU.min))
        vopE(lambda e: e.tensor_max(out=gs[:, :, :], in0=pmx[:, :, :, 0],
                                   in1=pmx[:, :, :, 1]))
        vopE(lambda e: e.tensor_max(out=pmn[:, :, :, 0], in0=pmn[:, :, :, 0],
                                   in1=pmn[:, :, :, 1]))
        vopE(lambda e: e.tensor_tensor(out=pmx[:, :, :, 0], in0=pmx[:, :, :, 0],
                                      in1=pmx[:, :, :, 1], op=ALU.min))
        vopE(lambda e: e.tensor_max(out=pmx[:, :, :, 0], in0=pmx[:, :, :, 0],
                                   in1=pmn[:, :, :, 0]))
        vopE(lambda e: e.tensor_add(out=gs[:, :, :], in0=gs[:, :, :],
                                   in1=pmx[:, :, :, 0]))
        vopE(lambda e: e.tensor_max(out=t_a, in0=gs[:, :, 0], in1=gs[:, :, 1]))
        vopE(lambda e: e.tensor_max(out=t_b, in0=gs[:, :, 2], in1=gs[:, :, 3]))
        vopE(lambda e: e.tensor_tensor(out=g2, in0=t_a, in1=t_b, op=ALU.min))
        vopE(lambda e: e.tensor_tensor(out=t_a, in0=gs[:, :, 0], in1=gs[:, :, 1],
                                      op=ALU.min))
        vopE(lambda e: e.tensor_tensor(out=t_b, in0=gs[:, :, 2], in1=gs[:, :, 3],
                                      op=ALU.min))
        vopE(lambda e: e.tensor_max(out=t_a, in0=t_a, in1=t_b))
        vopE(lambda e: e.tensor_max(out=g2, in0=g2, in1=t_a))
        g2b = g2.rearrange("p (b o) -> p b o", o=1).broadcast_to([128, B2, 4])
        vopE(lambda e: e.tensor_tensor(out=alw4[:, :, :], in0=gs[:, :, :],
                                      in1=g2b, op=ALU.is_ge))
        vopE(lambda e: e.tensor_scalar_add(out=alw4[:, :, :], in0=alw4[:, :, :],
                                          scalar1=-1.0))
        alwE = alw4.rearrange("p b (g o) -> p b g o", o=1).broadcast_to(
            [128, B2, 4, 4])
        vopE(lambda e: e.scalar_tensor_tensor(
            out=msk4[:, :, :, :], in0=alwE, scalar=BIG, in1=v4[:, :, :, :],
            op0=ALU.mult, op1=ALU.add))
        src = msk
        for it in range(4):
            vopE(lambda e, s=src: e.tensor_reduce(out=rmax, in_=s[:, :, :],
                                                 axis=AX.X, op=ALU.max))
            if it < 3:
                rmb = rmax.rearrange("p (b o) -> p b o", o=1).broadcast_to(
                    [128, B2, 16])
                vopE(lambda e, s=src, rb_=rmb: e.tensor_tensor(
                    out=eqm[:, :, :], in0=s[:, :, :], in1=rb_, op=ALU.is_ge))
                vopE(lambda e, s=src: e.scalar_tensor_tensor(
                    out=cur[:, :, :], in0=eqm[:, :, :], scalar=-BIG,
                    in1=s[:, :, :], op0=ALU.mult, op1=ALU.add))
                src = cur
        rmb = rmax.rearrange("p (b o) -> p b o", o=1).broadcast_to([128, B2, 16])
        vopE(lambda e: e.tensor_tensor(out=eqm[:, :, :], in0=msk[:, :, :],
                                      in1=rmb, op=ALU.is_ge))
        vopE(lambda e: e.tensor_mul(out=wsel[:, :, :], in0=sT3[:, :, :],
                                   in1=eqm[:, :, :]))
        vopE(lambda e: e.tensor_reduce(out=wsum, in_=wsel[:, :, :], axis=AX.X,
                                      op=ALU.add))
        vopE(lambda e: e.tensor_scalar_add(out=wsum, in0=wsum, scalar1=1e-6))
        wrc = vopE(lambda e: e.reciprocal(out=wsum, in_=wsum))
        wsb = wsum.rearrange("p (b o) -> p b o", o=1).broadcast_to([128, B2, 16])
        wcm = vopE(lambda e: e.tensor_tensor(out=wsel[:, :, :], in0=wsel[:, :, :],
                                            in1=wsb, op=ALU.mult), waits=[wrc])
        for tb in range(ntb1):
            bank = banks[7]
            transpose_into(bank, wselF[:, tb * 16:(tb + 1) * 16], ident,
                           waits=[wcm])
            read_bank(bank, "vector", lambda e,
                      o=comb16[0:16, t0 + tb * 128: t0 + (tb + 1) * 128],
                      p=bank.ps[0:16, 0:128]: e.tensor_copy(out=o, in_=p))
    S.fence([ve.w(), se.w(), pe.w(), st.w(), ld.w()])
    A.pop()

    routed = A.alloc(8 * TCORE, F32)
    A.push()
    wexp = [dict(w1=A.alloc(8 * FF, BF16), w3=A.alloc(8 * FF, BF16),
                 w2=A.alloc(4 * DM, BF16)) for _ in range(2)]
    gbf = A.alloc(4 * TT_C2, BF16)
    gubf2 = [A.alloc(4 * TT_C2, BF16), A.alloc(4 * TT_C2, BF16)]
    ubf = A.alloc(TT_C2, BF16)
    cb_sb = A.alloc(TT_C2, F32)

    stg_c2 = [A.alloc(512, F32) for _ in range(4)]
    stg_state = {"cast": [None] * 4}
    stg_sems = [lde0, lde1, ldpc, stpc]

    def load_expert(e_idx, slot, waits):
        sem = (lde0, lde1)[slot]
        ws = list(waits)
        jobs = []
        for part, src, rows, cols in ((0, w1e[e_idx], DM, FF),
                                      (1, w3e[e_idx], DM, FF),
                                      (2, w2e[e_idx], FF, DM)):
            dst = wexp[slot]["w1" if part == 0 else "w3" if part == 1 else "w2"]
            for kc in range(rows // 128):
                for c0 in range(0, cols, 512):
                    jobs.append((src[kc * 128:(kc + 1) * 128, c0:c0 + 512],
                                 dst[:, kc * cols + c0:kc * cols + c0 + 512]))
        last = None
        for j, (srcj, dstj) in enumerate(jobs):
            par = j % 4
            ssem = stg_sems[par]
            S.add("sync", lambda e, d=stg_c2[par], s2=srcj: e.dma_start(
                out=d, in_=s2),
                waits=(ws if j == 0 else []) +
                [w for w in (stg_state["cast"][par],) if w],
                inc=ssem, amt=16)
            wdma = ssem.w()
            last = vop(lambda e, o=dstj, i=stg_c2[par]: e.tensor_copy(
                out=o, in_=i), waits=[wdma])
            stg_state["cast"][par] = last
        return last

    wexp_ready = [None, None]
    wexp_free = [None, None]
    wexp_ready[0] = load_expert(0, 0, ())
    pending = None
    items = [(e, ti) for e in range(NE) for ti in range(NT_C2)]
    for it_i, (e_i, ti) in enumerate(items):
        slot = e_i % 2
        gpar = it_i % 2
        gubf = gubf2[gpar]
        wr = wexp_ready[slot]
        t0 = ti * TT_C2
        tw = TT_C2
        cbk = banks[2]
        mms(cbk, [(sel16[0:16, e_i * 128:(e_i + 1) * 128],
                   comb16[0:16, t0:t0 + tw])], tw)
        wcb = read_bank(cbk, "scalar", lambda e, o=cb_sb[:, 0:tw],
                        p=cbk.ps[:, 0:tw]: e.copy(out=o, in_=p),
                        waits=[buf_rd.get("cbw")])
        for fb in range(4):
            gb_, ub_ = (banks[0], banks[1]) if fb % 2 == 0 else (banks[5], banks[6])
            mms(gb_, [(wexp[slot]["w1"][:, kc * FF + fb * 128:
                                        kc * FF + (fb + 1) * 128],
                       xln2bf[:, kc * TCORE + t0: kc * TCORE + t0 + tw])
                      for kc in range(8)], tw, waits=[wr])
            wsilu = read_bank(gb_, "scalar", lambda e,
                              o=gbf[:, fb * tw:(fb + 1) * tw],
                              p=gb_.ps[:, 0:tw]: e.activation(
                                  out=o, in_=p, func=AF.Silu),
                              waits=[buf_rd.get("c2gbf")] if fb == 0 else ())
            mms(ub_, [(wexp[slot]["w3"][:, kc * FF + fb * 128:
                                        kc * FF + (fb + 1) * 128],
                       xln2bf[:, kc * TCORE + t0: kc * TCORE + t0 + tw])
                      for kc in range(8)], tw, waits=[wr])
            read_bank(ub_, "vector", lambda e, o=ubf[:, 0:tw],
                      p=ub_.ps[:, 0:tw], c=cb_sb[:, 0:tw]: e.tensor_mul(
                          out=o, in0=p, in1=c),
                      waits=[wcb] + ([buf_rd.get(("c2gu", gpar))]
                                     if fb == 0 else []))
            vop(lambda e, o=gubf[:, fb * tw:(fb + 1) * tw],
                i=gbf[:, fb * tw:(fb + 1) * tw], i1=ubf[:, 0:tw]:
                e.tensor_mul(out=o, in0=i, in1=i1), waits=[wsilu])
        buf_rd["cbw"] = ve.w()
        buf_rd["c2gbf"] = ve.w()
        gu_ready = ve.w()

        if pending is not None:
            pending()
        if ti == 0 and e_i + 1 < NE:
            wexp_ready[1 - slot] = load_expert(
                e_i + 1, 1 - slot,
                [wexp_free[1 - slot]] if wexp_free[1 - slot] else ())

        def make_w2(e_i=e_i, ti=ti, slot=slot, gpar=gpar, t0=t0,
                    gu_ready=gu_ready, gubf=gubf):
            def w2():
                tw = TT_C2
                for db in range(8):
                    bank = (banks[3], banks[4], banks[7])[db % 3]
                    mms(bank, [(wexp[slot]["w2"][:, kc * DM + db * 128:
                                                 kc * DM + (db + 1) * 128],
                                gubf[:, kc * tw:(kc + 1) * tw])
                               for kc in range(4)], tw, waits=[gu_ready])
                    if e_i == 0:
                        read_bank(bank, "vector", lambda e,
                                  o=routed[:, db * TCORE + t0:
                                           db * TCORE + t0 + tw],
                                  p=bank.ps[:, 0:tw]: e.tensor_copy(
                                      out=o, in_=p))
                    else:
                        read_bank(bank, "vector", lambda e,
                                  o=routed[:, db * TCORE + t0:
                                           db * TCORE + t0 + tw],
                                  p=bank.ps[:, 0:tw]: e.tensor_add(
                                      out=o, in0=o, in1=p))
                buf_rd[("c2gu", gpar)] = pe.w()
                if ti == NT_C2 - 1:
                    wexp_free[slot] = pe.w()
            return w2

        pending = make_w2()
    pending()
    S.fence([ve.w(), se.w(), pe.w(), st.w(), ld.w()])
    A.pop()

    A.push()
    w1s_bf = A.alloc(8 * FS, BF16)
    w3s_bf = A.alloc(8 * FS, BF16)
    w2s_bf = A.alloc(8 * DM, BF16)
    x2_t = A.alloc(8 * TT_C3, F32)
    stage_c3 = x2_t[:, 0:1024]
    gsbf = A.alloc(TT_C3, BF16)
    gush = A.alloc(8 * TT_C3, BF16)
    _otm = A.alloc(DM, F32)
    out_tm = [_otm, _otm]

    load_cast(w1s, DM, FS, w1s_bf, stage_c3, chunk_cols=1024)
    load_cast(w3s, DM, FS, w3s_bf, stage_c3, chunk_cols=1024)
    wsr = load_cast(w2s, FS, DM, w2s_bf, stage_c3, chunk_cols=1024)

    for ti in range(NT_C3):
        t0 = ti * TT_C3
        tw = TT_C3
        wx2 = None
        for a in range(8):
            wx2 = dma_in(x2_t[:, a * tw:(a + 1) * tw], sp_x2[a, :, t0:t0 + tw],
                         waits=(list(buf_rd.get("c3x2") or []) + [wsr]
                                if a == 0 else ()))
        wguprev = buf_rd.get("c3gu")
        for fb in range(8):
            gb_, ub_ = banks[0], banks[1]
            mms(gb_, [(w1s_bf[:, kc * FS + fb * 128: kc * FS + (fb + 1) * 128],
                       xln2bf[:, kc * TCORE + t0: kc * TCORE + t0 + tw])
                      for kc in range(8)], tw, waits=[wsr])
            wsl = read_bank(gb_, "scalar", lambda e, o=gsbf[:, 0:tw],
                            p=gb_.ps[:, 0:tw]: e.activation(
                                out=o, in_=p, func=AF.Silu),
                            waits=[wguprev] if fb == 0 and wguprev else ())
            mms(ub_, [(w3s_bf[:, kc * FS + fb * 128: kc * FS + (fb + 1) * 128],
                       xln2bf[:, kc * TCORE + t0: kc * TCORE + t0 + tw])
                      for kc in range(8)], tw, waits=[wsr])
            read_bank(ub_, "vector", lambda e, o=gush[:, fb * tw:(fb + 1) * tw],
                      i=gsbf[:, 0:tw], p=ub_.ps[:, 0:tw]: e.tensor_mul(
                          out=o, in0=i, in1=p),
                      waits=[wsl] + ([buf_rd.get("c3gu_pe")]
                                     if fb == 0 else []))
        gu_ready = ve.w()
        buf_rd["c3gu"] = gu_ready
        for db in range(8):
            bank = banks[3 + db % 2]
            mms(bank, [(w2s_bf[:, kc * DM + db * 128: kc * DM + (db + 1) * 128],
                        gush[:, kc * tw:(kc + 1) * tw]) for kc in range(8)],
                tw, waits=[gu_ready])
            vop(lambda e, o=x2_t[:, db * tw:(db + 1) * tw],
                i=routed[:, db * TCORE + t0: db * TCORE + t0 + tw]:
                e.tensor_add(out=o, in0=o, in1=i),
                waits=[wx2] if db == 0 else ())
            read_bank(bank, "vector", lambda e, o=x2_t[:, db * tw:(db + 1) * tw],
                      p=bank.ps[:, 0:tw]: e.tensor_add(out=o, in0=o, in1=p))
        buf_rd["c3gu_pe"] = pe.w()
        o_ready = ve.w()
        for tb in range(tw // 128):
            otm = out_tm[tb % 2]
            for db in range(8):
                bank = banks[6 + db % 2]
                transpose_into(bank,
                               x2_t[:, db * tw + tb * 128: db * tw + (tb + 1) * 128],
                               ident, waits=[o_ready])
                read_bank(bank, "scalar", lambda e,
                          o=otm[:, db * 128:(db + 1) * 128],
                          p=bank.ps[0:128, 0:128]: e.copy(out=o, in_=p),
                          waits=[sto.w()] if db == 0 else ())
            wtm = se.w()
            buf_rd[("otm", tb % 2)] = dma_out(
                out_t[t0 + tb * 128: t0 + (tb + 1) * 128, :], otm, waits=[wtm],
                sem=sto)
        buf_rd["c3x2"] = [ve.w(), pe.w()]
    A.pop()

    if debug:
        dbg_cmb = nc.dram_tensor("dbg_cmb", [16, TCORE], F32,
                                 kind="ExternalOutput").ap()
        dma_out(dbg_cmb, comb16[0:16, :], waits=[ve.w(), se.w(), pe.w()])
        dbg_xl2 = nc.dram_tensor("dbg_xl2", [128, 8 * TCORE], BF16,
                                 kind="ExternalOutput").ap()
        dma_out(dbg_xl2, xln2bf, waits=[ve.w(), se.w(), pe.w()])
        for nm, sp in (("dbg_xs", sp_xs), ("dbg_sz", sp_sz), ("dbg_xfm", sp_xfm),
                       ("dbg_y", sp_y), ("dbg_x2", sp_x2)):
            shp = list(sp.shape)
            dt_ = BF16 if nm in ("dbg_xs", "dbg_sz") else F32
            dbg = nc.dram_tensor(nm, shp, dt_, kind="ExternalOutput").ap()
            dma_out(dbg, sp, waits=[st.w(), ve.w(), se.w(), pe.w()])

    S.add("gpsimd", lambda e: e.nop(),
          waits=[st.w(), ld.w(), pe.w(), ve.w(), se.w(), st1.w(), sty.w(),
                 stx2.w(), sto.w(), ldg.w(), stsz.w(), stxs.w(), ldw.w(),
                 lde0.w(), lde1.w(), ldpc.w(), stpc.w()])

    with nc.Block() as block:
        @block.sync
        def _(eng):
            S.play("sync", eng)

        @block.tensor
        def _(eng):
            S.play("tensor", eng)

        @block.vector
        def _(eng):
            S.play("vector", eng)

        @block.scalar
        def _(eng):
            S.play("scalar", eng)

        @block.gpsimd
        def _(eng):
            S.play("gpsimd", eng)

    ctx.close()
    return nc


def kernel(**inputs):
    x = np.asarray(inputs["L0_attn_x"])[0]
    Wi = np.asarray(inputs["L0_attn_in_proj_w"])
    convw = np.asarray(inputs["L0_attn_conv_w"])[:, 0, :]
    convb = np.asarray(inputs["L0_attn_conv_b"])
    Wdt = np.asarray(inputs["L0_attn_dt_w"])
    dtb = np.asarray(inputs["L0_attn_dt_b"])
    WB = np.asarray(inputs["L0_attn_B_proj_w"])
    WC = np.asarray(inputs["L0_attn_C_proj_w"])
    Alog = np.asarray(inputs["L0_attn_A_log"])
    Dp = np.asarray(inputs["L0_attn_D_param"])
    Wo = np.asarray(inputs["L0_attn_out_proj_w"])
    Wg = np.asarray(inputs["L0_ffn_w_gate"])
    W1e = np.asarray(inputs["L0_ffn_w1_e"])
    W3e = np.asarray(inputs["L0_ffn_w3_e"])
    W2e = np.asarray(inputs["L0_ffn_w2_e"])
    W1s = np.asarray(inputs["L0_ffn_w1_s"])
    W3s = np.asarray(inputs["L0_ffn_w3_s"])
    W2s = np.asarray(inputs["L0_ffn_w2_s"])

    c = np.ascontiguousarray
    common = {
        "w_ip": c(Wi.T),
        "w_dt": c(Wdt.T),
        "w_bc": c(np.concatenate([WB.T, WC.T], axis=1)),
        "w_out": c(Wo.T),
        "w_gate": c(Wg.T),
        "w1e": c(W1e.transpose(0, 2, 1)),
        "w3e": c(W3e.transpose(0, 2, 1)),
        "w2e": c(W2e.transpose(0, 2, 1)),
        "w1s": c(W1s.T),
        "w3s": c(W3s.T),
        "w2s": c(W2s.T),
        "convw_r": c(convw.reshape(16, 128, KCONV).transpose(1, 0, 2)
                     .reshape(128, 16 * KCONV)),
        "convb_r": c(convb.reshape(16, 128).T),
        "dtb_r": c(dtb.reshape(16, 128).T),
        "alog_r": c(Alog.reshape(16, 128, DS).transpose(1, 0, 2)
                    .reshape(128, 16 * DS)),
        "dp_r": c(Dp.reshape(16, 128).T),
        "ident_in": np.eye(128, dtype=np.float32),
        "ones_row": np.ones((1, 128), dtype=np.float32),
        "ones_col_bf": np.ones((128, 1), dtype=BF),
        "fconst_in": np.tile(np.array([[1e-5, 1.0, -0.5, -1.0]],
                                      dtype=np.float32), (128, 1)),
    }
    sel16 = np.zeros((16, 16 * 128), dtype=np.float32)
    for n in range(16):
        sel16[n, n * 128:(n + 1) * 128] = 1.0
    common["sel16_in"] = sel16

    in_maps = []
    for ci in range(NCORES):
        lo = ci * TCORE
        xt = np.zeros((HALO + TCORE, DM), dtype=np.float32)
        s = max(0, lo - HALO)
        xt[HALO - (lo - s):] = x[s:lo + TCORE]
        sel8 = np.zeros((128, 8), dtype=np.float32)
        sel8[:, ci] = 1.0
        m = dict(common)
        m["x_tok"] = c(xt)
        m["sel8_in"] = sel8
        in_maps.append(m)

    nc = build_program()
    res = run_bass_kernel_spmd(nc, in_maps, core_ids=list(range(NCORES)))
    out = np.concatenate([np.asarray(r["out"]) for r in res.results], axis=0)
    return out.reshape(1, NCORES * TCORE, DM)
